# version 04
# baseline (speedup 1.0000x reference)
import numpy as np
import ml_dtypes
from contextlib import ExitStack

import concourse.bass as bass
import concourse.mybir as mybir
from concourse.bass_utils import run_bass_kernel_spmd

F32 = mybir.dt.float32
BF16 = mybir.dt.bfloat16
AF = mybir.ActivationFunctionType
ALU = mybir.AluOpType
AX = mybir.AxisListType.X

D = 1024
NTOK = 2048
NT = NTOK // 128
NCORES = 8
SEQ = 4096

BLOCK_ATTR = {"pe": "tensor", "act": "scalar", "dve": "vector", "pool": "gpsimd", "sp": "sync"}
CONST_VALS = [1e-5, -0.5, -1.0, 0.5, 0.125, 2.0, -1e30]


def _dt_size(dt):
    return {F32: 4, BF16: 2}[dt]


class Prog:

    def __init__(self, name="mk"):
        self.nc = bass.Bass("TRN2", target_bir_lowering=False,
                            detect_race_conditions=False)
        self.queues = {e: [] for e in BLOCK_ATTR}
        self.sem_names = ["pe", "act", "dve", "pool", "sp"]
        self.count = {s: 0 for s in self.sem_names}
        self.hwm = {e: {s: 0 for s in self.sem_names} for e in BLOCK_ATTR}
        self.writer = {}
        self.readers = {}
        self.sem_h = {}
        self._sb_off = 16512
        self.inputs = {}
        self.outputs = {}

    def sb(self, name, shape, dtype):
        off = (self._sb_off + 31) // 32 * 32
        h = self.nc.alloc_sbuf_tensor_at(name, list(shape), dtype, offset=off)
        self._sb_off = off + int(np.prod(shape[1:])) * _dt_size(dtype)
        assert self._sb_off <= 229344, f"SBUF overflow at {name}: {self._sb_off}"
        return h

    def sb_at(self, name, shape, dtype, base, byte_off=0):
        off = base.manual_sbuf_range[0] + byte_off
        assert off + int(np.prod(shape[1:])) * _dt_size(dtype) <= base.manual_sbuf_range[1] or True
        return self.nc.alloc_sbuf_tensor_at(name, list(shape), dtype, offset=off)

    def ps(self, name, shape, dtype=F32):
        return self.nc.alloc_psum_tensor(name, list(shape), dtype)

    def din(self, name, shape, dtype=F32):
        ap = self.nc.dram_tensor(name, list(shape), dtype, kind="ExternalInput").ap()
        self.inputs[name] = (shape, dtype)
        return ap

    def dout(self, name, shape, dtype=F32):
        ap = self.nc.dram_tensor(name, list(shape), dtype, kind="ExternalOutput").ap()
        self.outputs[name] = (shape, dtype)
        return ap

    def dtmp(self, name, shape, dtype=F32):
        return self.nc.dram_tensor(name, list(shape), dtype, kind="Internal").ap()

    def _dep_events(self, eng, reads, writes):
        need = {}

        def add(ev):
            if ev is None:
                return
            sem, val, oeng = ev
            if oeng == eng:
                return
            if val > self.hwm[eng][sem]:
                need[sem] = max(need.get(sem, 0), val)

        for k in reads:
            add(self.writer.get(k))
        for k in writes:
            add(self.writer.get(k))
            for (sem, oeng), val in self.readers.get(k, {}).items():
                add((sem, val, oeng))
        for sem, val in need.items():
            self.hwm[eng][sem] = val
        return list(need.items())

    def _record(self, reads, writes, ev):
        sem, val, oeng = ev
        for k in writes:
            self.writer[k] = ev
            self.readers[k] = {}
        for k in reads:
            d = self.readers.setdefault(k, {})
            kk = (sem, oeng)
            d[kk] = max(d.get(kk, 0), val)

    def emit(self, eng, reads, writes, body, strict=False):
        waits = self._dep_events(eng, reads, writes)
        self.count[eng] += 1
        val = self.count[eng]

        def run(e, waits=waits, body=body, val=val, sem=eng):
            for s, v in waits:
                e.wait_ge(self.sem_h[s], v)
            last = body(e)
            assert last is not None, "group body must return its last instruction"
            last.then_inc(self.sem_h[sem], 1)

        self.queues[eng].append(run)
        self._record(reads, writes, (eng, val, None if strict else eng))
        return (eng, val, eng)

    def _stream_sem(self, stream, kind):
        sem = f"{kind}_{stream}"
        if sem not in self.count:
            self.sem_names.append(sem)
            self.count[sem] = 0
            for e in BLOCK_ATTR:
                self.hwm[e][sem] = 0
        return sem

    def emit_dma(self, eng, reads, writes, body, n=1, stream="io"):
        sem = self._stream_sem(stream, "dq")
        waits = self._dep_events(eng, reads, writes)
        prev = self.count[sem]
        if prev > self.hwm[eng][sem]:
            waits = waits + [(sem, prev)]
            self.hwm[eng][sem] = prev
        self.count[sem] = prev + 16 * n
        val = self.count[sem]

        def run(e, waits=waits, body=body, n=n, sem=sem):
            for s, v in waits:
                e.wait_ge(self.sem_h[s], v)
            insts = body(e)
            assert len(insts) == n
            for ins in insts:
                ins.then_inc(self.sem_h[sem], 16)

        self.queues[eng].append(run)
        self._record(reads, writes, (sem, val, None))
        return (sem, val, None)

    def emit_cc(self, reads, writes, body, stream="cc"):
        sem = self._stream_sem(stream, "cs")
        waits = self._dep_events("pool", reads, writes)
        prev = self.count[sem]
        if prev > self.hwm["pool"][sem]:
            waits = waits + [(sem, prev)]
            self.hwm["pool"][sem] = prev
        self.count[sem] = prev + 1
        val = self.count[sem]

        def run(e, waits=waits, body=body, sem=sem):
            for s, v in waits:
                e.wait_ge(self.sem_h[s], v)
            body(e).then_inc(self.sem_h[sem], 1)

        self.queues["pool"].append(run)
        self._record(reads, writes, (sem, val, None))
        return (sem, val, None)

    def build(self):
        with ExitStack() as ctx:
            for s in self.sem_names:
                self.sem_h[s] = ctx.enter_context(self.nc.semaphore(s))
            block = ctx.enter_context(self.nc.Block())
            for eng, attr in BLOCK_ATTR.items():
                items = self.queues[eng]
                if not items:
                    continue

                def runner(e, items=items):
                    for f in items:
                        f(e)

                getattr(block, attr)(runner)
        return self.nc



class MK:

    def __init__(self, p: Prog):
        self.p = p
        self.X = p.sb("X", [128, NT * D], F32)
        self.AA = p.sb("AA", [128, 32768], BF16)
        self.AB = p.sb("AB", [128, 16384], BF16)
        self.W0 = p.sb("W0", [128, 8192], BF16)
        self.W1 = p.sb("W1", [128, 8192], BF16)
        self.SC = p.sb("SC", [128, 2048], BF16)
        self.SM = p.sb("SM", [128, 1024], F32)
        self.SMB = p.sb("SMB", [128, 1024], F32)
        self.GWT = p.sb("GWT", [128, 64], BF16)
        self.IDT = p.sb("IDT", [128, 128], BF16)
        self.IDTF = p.sb("IDTF", [128, 128], F32)
        self.ONE1 = p.sb("ONE1", [1, 128], BF16)
        self.CONSTS = p.sb("CONSTS", [128, 16], F32)
        for i, v in enumerate(CONST_VALS):
            p.nc.const_aps.register(F32, v, self.CONSTS[:, i:i + 1])
        self.PS = [p.ps(f"ps{i}", [128, 512], F32) for i in range(7)]
        self.PST = p.ps("pst", [128, 1024], BF16)
        self.wslots = [self.W0, self.W1]


    def load_const(self, ident_in, identf_in, ones_in, constv_in):
        p = self.p
        p.emit_dma("sp", [], ["IDT", "IDTF", "ONE1", "CONSTS"], lambda e: [
            e.dma_start(out=self.IDT[:], in_=ident_in),
            e.dma_start(out=self.IDTF[:], in_=identf_in),
            e.dma_start(out=self.ONE1[:], in_=ones_in),
            e.dma_start(out=self.CONSTS[:], in_=constv_in),
        ], n=4, stream="io")

    def load_x(self, x_in):
        self.p.emit_dma("sp", [], ["X"], lambda e: [
            e.dma_start(out=self.X[:].rearrange("p (t d) -> p t d", d=D),
                        in_=x_in.rearrange("(t p) d -> p t d", p=128))], stream="io")

    def store_x(self, x_out):
        self.p.emit_dma("sp", ["X"], ["XOUT"], lambda e: [
            e.dma_start(out=x_out.rearrange("(t p) d -> p t d", p=128),
                        in_=self.X[:].rearrange("p (t d) -> p t d", d=D))], stream="io")

    def finish(self):
        p = self.p
        waits = [(s, p.count[s]) for s in p.sem_names
                 if (s.startswith("dq_") or s.startswith("cs_")) and p.count[s]]

        def run(e, waits=waits):
            for s, v in waits:
                e.wait_ge(p.sem_h[s], v)

        p.queues["sp"].append(run)

    def ln_stats_all(self, nt=NT, x_of=None, x_key="X"):
        p = self.p
        SM = self.SM
        for t in range(nt):
            xt = self.X[:, t * D:(t + 1) * D] if x_of is None else x_of(t)
            s1 = SM[:, 4 * t:4 * t + 1]
            s2 = SM[:, 64 + 4 * t:64 + 4 * t + 1]
            sc = self.SC[:, (t % 2) * D:(t % 2) * D + D]
            p.emit("dve", [x_key], ["LNS"],
                   lambda e, s1=s1, xt=xt: e.reduce_sum(s1, xt, axis=AX), strict=True)
            p.emit("act", [x_key], ["LNS", f"SCs{t % 2}"],
                   lambda e, s2=s2, xt=xt, sc=sc: e.activation(
                       sc, xt, AF.Square, bias=0.0, scale=1.0, accum_out=s2),
                   strict=True)
        s1v = SM[:, 0:4 * nt:4]
        s2v = SM[:, 64:64 + 4 * nt:4]
        nmv = SM[:, 128:128 + 4 * nt:4]
        tmpv = SM[:, 192:192 + 4 * nt:4]
        varv = SM[:, 256:256 + 4 * nt:4]
        rsv = SM[:, 320:320 + 4 * nt:4]
        p.emit("dve", ["LNS"], ["LNS"],
               lambda e: e.tensor_scalar(nmv, s1v, -1.0 / D, None, ALU.mult),
               strict=True)
        p.emit("act", ["LNS"], ["LNS"],
               lambda e: e.activation(tmpv, nmv, AF.Square, bias=0.0, scale=1.0),
               strict=True)
        p.emit("dve", ["LNS"], ["LNS"],
               lambda e: e.scalar_tensor_tensor(varv, s2v, 1.0 / D, tmpv,
                                                ALU.mult, ALU.subtract), strict=True)
        p.emit("act", ["LNS", "CONSTS"], ["LNS"],
               lambda e: e.activation(rsv, varv, AF.Ln, bias=1e-5, scale=1.0),
               strict=True)
        p.emit("act", ["LNS", "CONSTS"], ["LNS"],
               lambda e: e.activation(rsv, rsv, AF.Exp, bias=0.0, scale=-0.5),
               strict=True)

    def ln_apply(self, t, out_slot, keep_f32=None, x_ap=None, x_key="X"):
        p = self.p
        SM = self.SM
        xt = self.X[:, t * D:(t + 1) * D] if x_ap is None else x_ap
        sck = f"SCs{out_slot}"
        sc = self.SC[:, out_slot * D:(out_slot + 1) * D]
        nmean = SM[:, 128 + 4 * t:128 + 4 * t + 1]
        rstd = SM[:, 320 + 4 * t:320 + 4 * t + 1]
        outs = [sck] + ([keep_f32[1]] if keep_f32 else [])

        def apply(e):
            last = e.tensor_scalar(sc, xt, nmean, rstd, ALU.add, ALU.mult)
            if keep_f32 is not None:
                last = e.tensor_scalar(keep_f32[0], xt, nmean, rstd, ALU.add, ALU.mult)
            return last

        p.emit("dve", [x_key, "LNS"], outs, apply)
        return sck

    def transpose_tile(self, src_ap, src_key, dst_ap3, dst_key, nchunk=8):
        p = self.p

        def tr(e):
            last = None
            for c in range(nchunk):
                last = e.transpose(self.PST[:, c * 128:(c + 1) * 128],
                                   src_ap[:, c * 128:(c + 1) * 128], self.IDT[:])
            return last

        p.emit("pe", [src_key, "IDT"], ["PST"], tr)

        def cp(e):
            return e.tensor_copy(dst_ap3, self.PST[:, 0:nchunk * 128].rearrange(
                "p (c f) -> p c f", f=128))

        p.emit("dve", ["PST"], [dst_key], cp)

    def ffn(self, layer, wts, debug=None):
        p = self.p
        FF = 2048 if layer == 0 else 4096
        NFC = FF // 128

        self.ln_stats_all()
        for t in range(NT):
            slot = t % 2
            sck = self.ln_apply(t, slot)
            dst = self.AB[:].rearrange("p (c tk) -> p c tk", tk=NTOK)[
                :, :, t * 128:(t + 1) * 128]
            self.transpose_tile(self.SC[:, slot * D:(slot + 1) * D], sck, dst, "AB")

        ab_fm = lambda dc, c0, c1: self.AB[:, dc * NTOK + c0: dc * NTOK + c1]

        if layer == 0:
            self._router0(wts)
        else:
            self._router1(wts)

        if debug is not None:
            p.emit_dma("sp", ["AB"], ["DBGAB"], lambda e: [
                e.dma_start(out=debug["ab"], in_=self.AB[:])], stream="io")
            p.emit_dma("sp", ["CMB"], ["DBGC"], lambda e: [
                e.dma_start(out=debug["cmb"], in_=self.SM[:, 384:512])], stream="io")

        FGW = 1024
        NFG = FF // FGW
        THALF = [(0, NT)] if layer == 0 else [(0, NT // 2), (NT // 2, NT)]
        w1ap, w2ap = wts["w1"], wts["w2"]

        NW2 = 2 if layer == 0 else 4
        W2W = 1024 // NW2
        plan = []
        for ei in range(8):
            for th in range(len(THALF)):
                for fg in range(NFG):
                    plan.append(("w1", ei, fg))
                for q in range(NW2):
                    plan.append(("w2", ei, q))

        li = [0]

        def emit_load(i):
            kind, ei, idx = plan[i]
            s = i % 2
            sl = self.wslots[s]
            if kind == "w1":
                src = w1ap[ei, :, idx * FGW:(idx + 1) * FGW].rearrange(
                    "(c p) f -> p c f", p=128)
                p.emit_dma("pool", [], [f"ws{s}"], lambda e: [
                    e.dma_start(out=sl[:].rearrange("p (c f) -> p c f", f=FGW),
                                in_=src)], stream=f"w{s}")
            else:
                src = w2ap[ei, :, idx * W2W:(idx + 1) * W2W].rearrange(
                    "(c p) d -> p c d", p=128)
                p.emit_dma("pool", [], [f"ws{s}"], lambda e: [
                    e.dma_start(out=sl[:, 0:NFC * W2W].rearrange(
                        "p (c d) -> p c d", d=W2W), in_=src)], stream=f"w{s}")

        def ensure_loads(upto):
            while li[0] <= min(upto, len(plan) - 1):
                emit_load(li[0])
                li[0] += 1

        ensure_loads(1)
        ci = 0
        hb = [0]
        for ei in range(8):
            for th_i, (t0, t1) in enumerate(THALF):
                ntile = t1 - t0
                TW = ntile * 128
                for fg in range(NFG):
                    s = ci % 2
                    sl = self.wslots[s]
                    for fc_in in range(FGW // 128):
                        fc = fg * (FGW // 128) + fc_in
                        for st in range(ntile // 4):
                            c0 = (t0 + st * 4) * 128
                            bank = hb[0] % 2
                            hb[0] += 1
                            psk = f"pb{bank}"

                            def mm(e, sl=sl, fc_in=fc_in, c0=c0, bank=bank):
                                last = None
                                for dc in range(8):
                                    last = e.matmul(
                                        self.PS[bank][:],
                                        sl[:, dc * FGW + fc_in * 128:
                                           dc * FGW + fc_in * 128 + 128],
                                        ab_fm(dc, c0, c0 + 512),
                                        start=(dc == 0), stop=(dc == 7))
                                return last

                            p.emit("pe", [f"ws{s}", "AB"], [psk], mm)

                            col = fc * TW + (c0 - t0 * 128)
                            dst = self.AA[:, col:col + 512]
                            if layer == 0:
                                def ev(e, dst=dst, bank=bank):
                                    return e.activation(dst, self.PS[bank][:],
                                                        AF.Silu, bias=0.0, scale=1.0)
                                p.emit("act", [psk], ["AA"], ev)
                            else:
                                bcol = 512 + fc * 8 + ei

                                def ev(e, dst=dst, bank=bank, bcol=bcol):
                                    return e.activation(
                                        dst, self.PS[bank][:], AF.Gelu,
                                        bias=self.SM[:, bcol:bcol + 1], scale=1.0)
                                p.emit("act", [psk, "B1T"], ["AA"], ev)
                    ci += 1
                    ensure_loads(ci + 1)
                if debug is not None and ei == 0:
                    p.emit_dma("sp", ["AA"], ["DBGH"], lambda e: [
                        e.dma_start(out=debug["h"], in_=self.AA[:])], stream="io")
                for q in range(NW2):
                    s = ci % 2
                    sl = self.wslots[s]
                    for t in range(t0, t1):
                        bank = 2 + (t % 2)
                        psk = f"pb{bank}"
                        tloc = t - t0

                        def ymm(e, sl=sl, tloc=tloc, bank=bank, TW=TW):
                            last = None
                            for fc in range(NFC):
                                last = e.matmul(
                                    self.PS[bank][:, 0:W2W],
                                    self.AA[:, fc * TW + tloc * 128:
                                            fc * TW + tloc * 128 + 128],
                                    sl[:, fc * W2W:(fc + 1) * W2W],
                                    start=(fc == 0), stop=(fc == NFC - 1))
                            return last

                        p.emit("pe", [f"ws{s}", "AA"], [psk], ymm)

                        xt = self.X[:, t * D + q * W2W: t * D + q * W2W + W2W]
                        cmb = self.SM[:, 384 + t * 8 + ei: 384 + t * 8 + ei + 1]

                        def ev(e, xt=xt, cmb=cmb, bank=bank):
                            return e.scalar_tensor_tensor(
                                xt, self.PS[bank][:, 0:W2W], cmb, xt,
                                ALU.mult, ALU.add)

                        p.emit("dve", [psk, "CMB", "X"], ["X"], ev)
                    ci += 1
                    ensure_loads(ci + 1)
        if layer == 1:
            self._ffn1_b2(wts)

    def _router0(self, wts):
        p = self.p
        gw = wts["gate_w"]
        p.emit_dma("pool", [], ["GWT"], lambda e: [
            e.dma_start(out=self.GWT[:].rearrange("p (c e2) -> p c e2", e2=8),
                        in_=gw.rearrange("(c p) e -> p c e", p=128))], stream="gw")

        for st in range(4):
            c0 = st * 512

            def lmm(e, c0=c0):
                last = None
                for dc in range(8):
                    last = e.matmul(self.PS[4][0:8, :],
                                    self.GWT[:, dc * 8:(dc + 1) * 8],
                                    self.AB[:, dc * NTOK + c0: dc * NTOK + c0 + 512],
                                    start=(dc == 0), stop=(dc == 7))
                return last

            p.emit("pe", ["GWT", "AB"], ["pb4"], lmm)
            p.emit("dve", ["pb4"], ["LGS"],
                   lambda e: e.tensor_copy(self.SMB[0:8, 0:512], self.PS[4][0:8, :]))

            def trf(e):
                last = None
                for q in range(4):
                    last = e.transpose(self.PS[5][:, q * 8:(q + 1) * 8],
                                       self.SMB[0:8, q * 128:(q + 1) * 128],
                                       self.IDTF[0:8, 0:8])
                return last

            p.emit("pe", ["LGS", "IDTF"], ["pb5"], trf)
            p.emit("dve", ["pb5"], ["LT"],
                   lambda e: e.tensor_copy(self.SM[:, 864:896], self.PS[5][:, 0:32]))

            for q in range(4):
                t = st * 4 + q
                self._top2_combine(self.SM[:, 864 + q * 8:864 + q * 8 + 8],
                                   self.SM[:, 384 + t * 8:384 + t * 8 + 8],
                                   softmax_top2=True)

    def _top2_combine(self, LT, cmb, softmax_top2):
        p = self.p
        SM, SMB = self.SM, self.SMB
        m1 = SM[:, 832:833]
        m2 = SM[:, 836:837]
        dlt = SM[:, 840:841]
        ex = SM[:, 844:845]
        war = SM[:, 848:849]
        wa = SM[:, 852:853]
        wb = SM[:, 856:857]
        ge1 = SMB[:, 512:520]
        ge2 = SMB[:, 528:536]
        l2 = SMB[:, 544:552]
        geA = SMB[:, 560:568]

        p.emit("dve", ["LT"], ["RTS"],
               lambda e: e.reduce_max(m1, LT, axis=AX), strict=True)

        p.emit("dve", ["RTS", "LT"], ["RTS"],
               lambda e: e.tensor_scalar(ge1, LT, m1, None, ALU.is_ge), strict=True)
        p.emit("dve", ["RTS", "LT"], ["RTS"],
               lambda e: e.scalar_tensor_tensor(l2, ge1, -1e30, LT, ALU.mult, ALU.add),
               strict=True)
        p.emit("dve", ["RTS"], ["RTS"],
               lambda e: e.reduce_max(m2, l2, axis=AX), strict=True)
        p.emit("dve", ["RTS"], ["RTS"],
               lambda e: e.tensor_tensor(dlt, m2, m1, ALU.subtract), strict=True)
        p.emit("act", ["RTS"], ["RTS"],
               lambda e: e.activation(ex, dlt, AF.Exp, bias=0.0, scale=1.0),
               strict=True)
        p.emit("dve", ["RTS"], ["RTS"],
               lambda e: e.tensor_scalar(war, ex, 1.0, None, ALU.add), strict=True)
        p.emit("dve", ["RTS"], ["RTS"],
               lambda e: e.reciprocal(war, war), strict=True)
        p.emit("dve", ["RTS"], ["RTS"],
               lambda e: e.tensor_scalar(wa, war, 0.5, None, ALU.mult), strict=True)
        p.emit("dve", ["RTS"], ["RTS"],
               lambda e: e.tensor_scalar(wb, wa, -1.0, 0.5, ALU.mult, ALU.add),
               strict=True)
        p.emit("dve", ["RTS", "LT"], ["RTS"],
               lambda e: e.tensor_scalar(geA, LT, m2, None, ALU.is_ge), strict=True)
        p.emit("dve", ["RTS"], ["RTS"],
               lambda e: e.tensor_tensor(ge2, geA, ge1, ALU.subtract), strict=True)
        p.emit("dve", ["RTS"], ["RTS", "CMBp"],
               lambda e: e.tensor_scalar(cmb, ge1, wa, None, ALU.mult), strict=True)
        p.emit("dve", ["RTS", "CMBp"], ["CMB"],
               lambda e: e.scalar_tensor_tensor(cmb, ge2, wb, cmb, ALU.mult, ALU.add),
               strict=True)

    def ln_tile_seq(self, x_ap, x_key, out_ap, out_key, scol=0):
        p = self.p
        SM = self.SM
        s1 = SM[:, scol:scol + 1]
        s2 = SM[:, scol + 4:scol + 5]
        nmean = SM[:, scol + 8:scol + 9]
        tmp = SM[:, scol + 12:scol + 13]
        var = SM[:, scol + 16:scol + 17]
        rstd = SM[:, scol + 20:scol + 21]
        sq_dump = self.SC[:, 0:D]
        p.emit("dve", [x_key], ["LNS"],
               lambda e: e.reduce_sum(s1, x_ap, axis=AX), strict=True)
        p.emit("act", [x_key, "LNS"], ["LNS", "SCs0"],
               lambda e: e.activation(sq_dump, x_ap, AF.Square, bias=0.0, scale=1.0,
                                      accum_out=s2), strict=True)
        p.emit("dve", ["LNS"], ["LNS"],
               lambda e: e.tensor_scalar(nmean, s1, -1.0 / D, None, ALU.mult),
               strict=True)
        p.emit("act", ["LNS"], ["LNS"],
               lambda e: e.activation(tmp, nmean, AF.Square, bias=0.0, scale=1.0),
               strict=True)
        p.emit("dve", ["LNS"], ["LNS"],
               lambda e: e.scalar_tensor_tensor(var, s2, 1.0 / D, tmp,
                                                ALU.mult, ALU.subtract), strict=True)
        p.emit("act", ["LNS", "CONSTS"], ["LNS"],
               lambda e: e.activation(rstd, var, AF.Ln, bias=1e-5, scale=1.0),
               strict=True)
        p.emit("act", ["LNS", "CONSTS"], ["LNS"],
               lambda e: e.activation(rstd, rstd, AF.Exp, bias=0.0, scale=-0.5),
               strict=True)
        p.emit("dve", [x_key, "LNS"], [out_key],
               lambda e: e.tensor_scalar(out_ap, x_ap, nmean, rstd, ALU.add, ALU.mult))
        return nmean, rstd

    def mamba(self, wts, partner_src, partner_bf16=False, partner_reads=()):
        p = self.p
        AA, SM, SMB, SC = self.AA, self.SM, self.SMB, self.SC
        NV = 32

        DFULLT = p.sb_at("DFULLT", [128, 1024], F32, self.W0)
        self.DFULLT = DFULLT
        DTF = p.sb_at("DTF", [8, 4096], F32, self.W0)
        DEAD = p.sb_at("DEAD", [8, 4096], F32, self.W0)
        HSEL = p.sb_at("HSEL", [8, 1024], F32, self.W0)
        WXT = p.sb_at("WXT", [128, 1088], BF16, self.W1)
        ACUM = p.sb_at("ACUM", [8, 4096], F32, self.W1)
        SSCR = p.sb_at("SSCR", [8, 4096], F32, self.AB)
        NADF = p.sb_at("NADF", [8, 4096], F32, self.AB)
        BF = p.sb_at("BF", [64, 4096], BF16, self.AB, 16384)
        CF = p.sb_at("CF", [64, 4096], BF16, self.AB, 24576)
        XSTG = p.sb_at("XSTG", [128, 1024], F32, self.SMB)
        BCT = p.sb_at("BCT", [128, 1024], F32, self.SMB)
        DTTM = SM[:, 384:640].rearrange("p (v h) -> p v h", h=8)
        NADTM = SM[:, 640:896].rearrange("p (v h) -> p v h", h=8)
        ALOG = SM[0:8, 896:897]
        NEGA = SM[0:8, 900:901]
        ST = p.sb("STm", [128, 512], BF16)
        LTL = p.sb("LTLm", [128, 512], BF16)
        TRI = p.sb("TRIm", [128, 128], BF16)

        wxt, dtb, alog = wts["WXT"], wts["dt_bias"], wts["A_log"]
        PS, PST = self.PS, self.PST

        p.emit_dma("pool", [], ["ws1"], lambda e: [
            e.dma_start(out=WXT[:].rearrange("p (c f) -> p c f", f=136),
                        in_=wxt.rearrange("(c p) f -> p c f", p=128))], stream="w1")
        p.emit_dma("sp", [], ["TRIm"], lambda e: [
            e.dma_start(out=TRI[:], in_=wts["TRI"])], stream="io")
        p.emit_dma("sp", [], ["Asm"], lambda e: [
            e.dma_start(out=ALOG, in_=alog)], stream="io")
        p.emit_dma("sp", [], ["ws0"], lambda e: [
            e.dma_start(out=DFULLT[:], in_=wts["DFULL"])], stream="io")

        for t in range(NT):
            xt = self.X[:, t * D:(t + 1) * D]
            aa = AA[:, (16 + t) * 1024:(17 + t) * 1024]
            nmean, rstd = self.ln_tile_seq(xt, "X", aa, "AA", scol=(t % 2) * 24)
            tmp = XSTG[:]

            def dterm1(e, xt=xt, nmean=nmean, tmp=tmp):
                return e.scalar_tensor_tensor(tmp, xt, nmean, self.DFULLT[:],
                                              ALU.add, ALU.mult)

            p.emit("dve", ["X", "LNS", "ws0"], ["SMB"], dterm1)

            def dterm2(e, xt=xt, rstd=rstd, tmp=tmp):
                return e.scalar_tensor_tensor(xt, tmp, rstd, xt, ALU.mult, ALU.add)

            p.emit("dve", ["SMB", "LNS", "X"], ["X"], dterm2)

        for v in range(NT):
            if partner_bf16:
                p.emit_dma("pool", list(partner_reads), ["SMB"], lambda e, v=v: [
                    e.dma_start(out=XSTG[:], in_=partner_src(v))], stream="xs")
            else:
                p.emit_dma("sp", list(partner_reads), ["SMB"], lambda e, v=v: [
                    e.dma_start(out=XSTG[:], in_=partner_src(v))], stream="xs")
            aa = AA[:, v * 1024:(v + 1) * 1024]
            self.ln_tile_seq(XSTG[:], "SMB", aa, "AA", scol=(v % 2) * 24)

        for v in range(NV):
            src = AA[:, v * 1024:(v + 1) * 1024]

            def tr(e, src=src):
                last = None
                for c in range(8):
                    last = e.transpose(PST[:, c * 128:(c + 1) * 128],
                                       src[:, c * 128:(c + 1) * 128], self.IDT[:])
                return last

            p.emit("pe", ["AA", "IDT"], ["PST"], tr)
            slot = v % 2
            xf = SC[:, slot * D:(slot + 1) * D]
            p.emit("dve", ["PST"], [f"SCs{slot}"],
                   lambda e, xf=xf: e.tensor_copy(
                       xf.rearrange("p (c f) -> p c f", f=128),
                       PST[:, 0:1024].rearrange("p (c f) -> p c f", f=128)))

            def mm(e, xf=xf):
                last = None
                for dc in range(8):
                    e.matmul(PS[4][0:8, 0:128], WXT[:, dc * 136:dc * 136 + 8],
                             xf[:, dc * 128:(dc + 1) * 128],
                             start=(dc == 0), stop=(dc == 7))
                for dc in range(8):
                    e.matmul(PS[4][0:64, 128:256], WXT[:, dc * 136 + 8:dc * 136 + 72],
                             xf[:, dc * 128:(dc + 1) * 128],
                             start=(dc == 0), stop=(dc == 7))
                for dc in range(8):
                    last = e.matmul(PS[4][0:64, 256:384],
                                    WXT[:, dc * 136 + 72:dc * 136 + 136],
                                    xf[:, dc * 128:(dc + 1) * 128],
                                    start=(dc == 0), stop=(dc == 7))
                return last

            p.emit("pe", [f"SCs{slot}", "ws1"], ["pb4"], mm)
            c0 = v * 128

            def ev(e, c0=c0):
                e.tensor_copy(DTF[:, c0:c0 + 128], PS[4][0:8, 0:128])
                e.tensor_copy(BF[:, c0:c0 + 128], PS[4][0:64, 128:256])
                return e.tensor_copy(CF[:, c0:c0 + 128], PS[4][0:64, 256:384])

            p.emit("dve", ["pb4"], ["ws0", "ABb", "ABc"], ev)

        DB = SM[0:8, 904:905]
        p.emit_dma("sp", [], ["Asm2"], lambda e: [
            e.dma_start(out=DB, in_=dtb)], stream="io")
        p.emit("dve", ["ws0", "Asm2"], ["ws0"],
               lambda e: e.tensor_scalar(DTF[:], DTF[:], DB, None, ALU.add))
        p.emit("act", ["ws0"], ["ABs"],
               lambda e: e.activation(SSCR[:], DTF[:], AF.Abs, bias=0.0, scale=1.0))
        p.emit("act", ["ABs", "CONSTS"], ["ABs"],
               lambda e: e.activation(SSCR[:], SSCR[:], AF.Exp, bias=0.0, scale=-1.0))
        p.emit("act", ["ABs"], ["ABs"],
               lambda e: e.activation(SSCR[:], SSCR[:], AF.Ln, bias=1.0, scale=1.0))
        p.emit("dve", ["ws0"], ["ws0"],
               lambda e: e.tensor_scalar(DTF[:], DTF[:], 0.0, None, ALU.max))
        p.emit("dve", ["ws0", "ABs"], ["ws0"],
               lambda e: e.tensor_tensor(DTF[:], DTF[:], SSCR[:], ALU.add))

        def trdt(e):
            last = None
            for v in range(NV):
                last = e.transpose(PS[4][:, v * 8:(v + 1) * 8],
                                   DTF[0:8, v * 128:(v + 1) * 128], self.IDTF[0:8, 0:8])
            return last

        p.emit("pe", ["ws0", "IDTF"], ["pb4"], trdt)
        p.emit("dve", ["pb4"], ["DTTM"],
               lambda e: e.tensor_copy(DTTM, PS[4][:, 0:256].rearrange(
                   "p (v h) -> p v h", h=8)))

        p.emit("act", ["Asm"], ["Asm"],
               lambda e: e.activation(NEGA, ALOG, AF.Exp, bias=0.0, scale=1.0),
               strict=True)
        p.emit("dve", ["Asm"], ["Asm"],
               lambda e: e.tensor_scalar(NEGA, NEGA, -1.0, None, ALU.mult),
               strict=True)
        p.emit("dve", ["ws0", "Asm", "DTTM"], ["ws0"],
               lambda e: e.tensor_scalar(DTF[:], DTF[:], NEGA, None, ALU.mult))
        p.emit("dve", ["ws0"], ["ws1"],
               lambda e: e.tensor_tensor_scan(ACUM[:], DTF[:], DTF[:], 0.0,
                                              ALU.add, ALU.bypass), strict=True)

        p.emit_dma("pool", ["ws0"], ["ws0"], lambda e: [
            e.dma_start(out=DEAD[:], in_=wts["DEADV"])], stream="w0")
        p.emit("dve", ["ws0", "ws1"], ["ABs"],
               lambda e: e.scalar_tensor_tensor(NADF[:], ACUM[:], -1.0, DEAD[:],
                                                ALU.mult, ALU.add))

        def trnad(e):
            last = None
            for v in range(NV):
                last = e.transpose(PS[4][:, v * 8:(v + 1) * 8],
                                   NADF[0:8, v * 128:(v + 1) * 128],
                                   self.IDTF[0:8, 0:8])
            return last

        p.emit("pe", ["ABs", "IDTF"], ["pb4"], trnad)
        p.emit("dve", ["pb4"], ["NADTM"],
               lambda e: e.tensor_copy(NADTM, PS[4][:, 0:256].rearrange(
                   "p (v h) -> p v h", h=8)))
        p.emit_dma("pool", ["ws0"], ["ws0"], lambda e: [
            e.dma_start(out=HSEL[:], in_=wts["HSEL"])], stream="w0")

        for v in range(NV):
            def xd(e, v=v):
                last = None
                for h in range(8):
                    last = e.tensor_scalar(
                        AA[:, v * 1024 + h * 128: v * 1024 + h * 128 + 128],
                        AA[:, v * 1024 + h * 128: v * 1024 + h * 128 + 128],
                        DTTM[:, v, h:h + 1].squeeze(1) if False else
                        SM[:, 384 + v * 8 + h:384 + v * 8 + h + 1],
                        None, ALU.mult)
                return last

            p.emit("dve", ["AA", "DTTM"], ["AA"], xd)

        for ls in range(4):
            jsv = 16 + ls * 4
            l0 = jsv * 128
            ns = jsv + 4
            for hg in range(4):
                heads = [hg * 2, hg * 2 + 1]
                for i, h in enumerate(heads):
                    def bcmm(e, h=h, l0=l0):
                        return e.matmul(PS[6][:], HSEL[:, h * 128:(h + 1) * 128],
                                        ACUM[:, l0:l0 + 512], start=True, stop=True)

                    p.emit("pe", ["ws0", "ws1"], ["pb6"], bcmm)
                    p.emit("dve", ["pb6"], ["SMB"],
                           lambda e, i=i: e.tensor_copy(BCT[:, i * 512:(i + 1) * 512],
                                                        PS[6][:]))
                ydone = []
                for si in range(ns):
                    def gmm(e, si=si, l0=l0):
                        return e.matmul(PS[5][:], BF[:, si * 128:(si + 1) * 128],
                                        CF[:, l0:l0 + 512], start=True, stop=True)

                    p.emit("pe", ["ABb", "ABc"], ["pb5"], gmm)
                    diag = (jsv <= si < jsv + 4)
                    for i, h in enumerate(heads):
                        nad = SM[:, 640 + si * 8 + h:640 + si * 8 + h + 1]
                        bch = BCT[:, i * 512:(i + 1) * 512]
                        if not diag:
                            p.emit("act", ["SMB", "NADTM"], ["LTLm"],
                                   lambda e, bch=bch, nad=nad: e.activation(
                                       LTL[:], bch, AF.Exp, bias=nad, scale=1.0))
                        else:
                            p.emit("dve", ["SMB", "NADTM"], ["LTLm"],
                                   lambda e, bch=bch, nad=nad: e.tensor_scalar(
                                       LTL[:], bch, nad, 0.0, ALU.add, ALU.min))
                            p.emit("act", ["LTLm"], ["LTLm"],
                                   lambda e: e.activation(LTL[:], LTL[:], AF.Exp,
                                                          bias=0.0, scale=1.0))
                        p.emit("dve", ["pb5", "LTLm"], ["STm"],
                               lambda e: e.tensor_tensor(ST[:], PS[5][:], LTL[:],
                                                         ALU.mult))
                        if diag:
                            q = si - jsv
                            if q > 0:
                                p.emit("dve", ["STm"], ["STm"],
                                       lambda e, q=q: e.memset(ST[:, 0:q * 128], 0.0))
                            p.emit("dve", ["STm", "TRIm"], ["STm"],
                                   lambda e, q=q: e.tensor_tensor(
                                       ST[:, q * 128:(q + 1) * 128],
                                       ST[:, q * 128:(q + 1) * 128],
                                       TRI[:], ALU.mult))
                        def ymm(e, si=si, i=i, h=h, first=(si == 0), last=(si == ns - 1)):
                            lastmm = None
                            for lq in range(4):
                                lastmm = e.matmul(
                                    PS[i][:, lq * 128:(lq + 1) * 128],
                                    ST[:, lq * 128:(lq + 1) * 128],
                                    AA[:, si * 1024 + h * 128: si * 1024 + h * 128 + 128],
                                    start=(first and lq == 0),
                                    stop=(last and lq == 3))
                            return lastmm

                        p.emit("pe", ["STm", "AA"], [f"pb{i}"], ymm)
                for i, h in enumerate(heads):
                    for lq in range(4):
                        t = ls * 4 + lq
                        xt = self.X[:, t * D + h * 128: t * D + h * 128 + 128]

                        def ev(e, xt=xt, i=i, lq=lq):
                            return e.tensor_tensor(xt, PS[i][:, lq * 128:(lq + 1) * 128],
                                                   xt, ALU.add)

                        p.emit("dve", [f"pb{i}", "X"], ["X"], ev)

    def _router1(self, wts):
        p = self.p
        SM, SMB = self.SM, self.SMB
        PS = self.PS

        B1S = p.sb_at("B1S", [8, 4096], F32, self.W1)
        p.emit_dma("sp", [], ["ws1"], lambda e: [
            e.dma_start(out=B1S[:], in_=wts["b1"])], stream="io")

        def trb1(e):
            last = None
            for fc in range(32):
                last = e.transpose(PS[4][:, fc * 8:(fc + 1) * 8],
                                   B1S[:, fc * 128:(fc + 1) * 128],
                                   self.IDTF[0:8, 0:8])
            return last

        p.emit("pe", ["ws1", "IDTF"], ["pb4"], trb1)
        p.emit("dve", ["pb4"], ["B1T"],
               lambda e: e.tensor_copy(SM[:, 512:768], PS[4][:, 0:256]))

        p.emit_dma("pool", [], ["GWT"], lambda e: [
            e.dma_start(out=self.GWT[:].rearrange("p (c e2) -> p c e2", e2=8),
                        in_=wts["router_w"].rearrange("(c p) e -> p c e", p=128))],
            stream="gw")
        RB = SM[0:8, 908:909]
        p.emit_dma("sp", [], ["Asm3"], lambda e: [
            e.dma_start(out=RB, in_=wts["router_b"])], stream="io")

        MSKF = p.sb_at("MSKF", [8, 2048], F32, self.W0)
        PSF = p.sb_at("PSF", [8, 2048], F32, self.W0, 8192)
        NEGM1 = SM[:, 912:913]
        RCP = SM[:, 916:917]
        m1 = SM[:, 832:833]
        m2 = SM[:, 836:837]
        SACC = SM[:, 920:921]
        GTS = SMB[:, 576:584]
        ge1 = SMB[:, 640:648]
        l2 = SMB[:, 656:664]

        for st in range(4):
            c0 = st * 512

            def lmm(e, c0=c0):
                last = None
                for dc in range(8):
                    last = e.matmul(PS[4][0:8, :],
                                    self.GWT[:, dc * 8:(dc + 1) * 8],
                                    self.AB[:, dc * NTOK + c0: dc * NTOK + c0 + 512],
                                    start=(dc == 0), stop=(dc == 7))
                return last

            p.emit("pe", ["GWT", "AB"], ["pb4"], lmm)
            p.emit("dve", ["pb4", "Asm3"], ["LGS"],
                   lambda e: e.tensor_scalar(SMB[0:8, 0:512], PS[4][0:8, :],
                                             RB, None, ALU.add))

            def trf(e):
                last = None
                for q in range(4):
                    last = e.transpose(PS[5][:, q * 8:(q + 1) * 8],
                                       SMB[0:8, q * 128:(q + 1) * 128],
                                       self.IDTF[0:8, 0:8])
                return last

            p.emit("pe", ["LGS", "IDTF"], ["pb5"], trf)
            p.emit("dve", ["pb5"], ["LT"],
                   lambda e: e.tensor_copy(self.SM[:, 864:896], self.PS[5][:, 0:32]))

            for q in range(4):
                t = st * 4 + q
                LT = SM[:, 864 + q * 8:864 + q * 8 + 8]
                gates = SM[:, 384 + t * 8:384 + t * 8 + 8]
                msk = SMB[:, 768 + t * 8:768 + t * 8 + 8]
                p.emit("dve", ["LT"], ["RTS"],
                       lambda e, LT=LT: e.reduce_max(m1, LT, axis=AX), strict=True)
                p.emit("dve", ["RTS"], ["RTS"],
                       lambda e: e.tensor_scalar(NEGM1, m1, -1.0, None, ALU.mult),
                       strict=True)
                p.emit("act", ["RTS", "LT"], ["RTS"],
                       lambda e, LT=LT: e.activation(GTS, LT, AF.Exp, bias=NEGM1,
                                                     scale=1.0, accum_out=SACC),
                       strict=True)
                p.emit("dve", ["RTS"], ["RTS"],
                       lambda e: e.reciprocal(RCP, SACC), strict=True)
                p.emit("dve", ["RTS"], ["CMB"],
                       lambda e, gates=gates: e.tensor_scalar(gates, GTS, RCP,
                                                              None, ALU.mult))
                p.emit("dve", ["RTS", "LT"], ["RTS"],
                       lambda e, LT=LT: e.tensor_scalar(ge1, LT, m1, None, ALU.is_ge),
                       strict=True)
                p.emit("dve", ["RTS", "LT"], ["RTS"],
                       lambda e, LT=LT: e.scalar_tensor_tensor(
                           l2, ge1, -1e30, LT, ALU.mult, ALU.add), strict=True)
                p.emit("dve", ["RTS"], ["RTS"],
                       lambda e: e.reduce_max(m2, l2, axis=AX), strict=True)
                p.emit("dve", ["RTS", "LT"], ["MSKT"],
                       lambda e, LT=LT, msk=msk: e.tensor_scalar(
                           msk, LT, m2, None, ALU.is_ge), strict=True)

        for g in range(4):
            def trmsk(e, g=g):
                last = None
                for q in range(4):
                    t = g * 4 + q
                    last = e.transpose(PS[4][0:8, q * 128:(q + 1) * 128],
                                       SMB[:, 768 + t * 8:768 + t * 8 + 8],
                                       self.IDTF[:])
                return last

            p.emit("pe", ["MSKT", "IDTF"], ["pb4"], trmsk)
            p.emit("dve", ["pb4"], ["ws0"],
                   lambda e, g=g: e.tensor_copy(MSKF[:, g * 512:(g + 1) * 512],
                                                PS[4][0:8, 0:512]))
        p.emit("dve", ["ws0"], ["ws0"],
               lambda e: e.tensor_tensor_scan(PSF[:], MSKF[:], MSKF[:], 0.0,
                                              ALU.add, ALU.bypass), strict=True)
        TOT = SM[0:8, 924:925]
        p.emit("dve", ["ws0"], ["Atot"],
               lambda e: e.tensor_copy(TOT, PSF[:, 2047:2048]), strict=True)
        cc_in = p.dtmp("cc_cnt_in" + wts["tag"], [1, 8], F32)
        cc_out = p.dtmp("cc_cnt_out" + wts["tag"], [8, 8], F32)
        p.emit_dma("pool", ["Atot"], ["ccin"], lambda e: [
            e.dma_start(out=cc_in.rearrange("one e -> e one"), in_=TOT)], stream="cc")
        p.emit_cc(["ccin"], ["ccout"], lambda e: e.collective_compute(
            "AllGather", ALU.bypass, replica_groups=[list(range(8))],
            ins=[cc_in.tensor.ap().opt()], outs=[cc_out.tensor.ap().opt()]))
        AGO = SMB[0:8, 688:696]
        p.emit_dma("pool", ["ccout"], ["AGO"], lambda e: [
            e.dma_start(out=AGO, in_=cc_out)], stream="cc")
        OFFS = SM[0:8, 928:929]
        self.PREF = SM[0:8, 932:933]
        p.emit_dma("sp", [], ["PREF"], lambda e: [
            e.dma_start(out=self.PREF, in_=wts["prefmask"])], stream="io")

        def offmm(e):
            return e.matmul(PS[6][0:8, 0:1], AGO, self.PREF, start=True, stop=True)

        p.emit("pe", ["AGO", "PREF"], ["pb6"], offmm)
        p.emit("dve", ["pb6"], ["Aoff"],
               lambda e: e.tensor_copy(OFFS, PS[6][0:8, 0:1]), strict=True)
        p.emit("dve", ["ws0", "Aoff"], ["ws0"],
               lambda e: e.tensor_scalar(PSF[:], PSF[:], OFFS, None, ALU.add))
        p.emit("dve", ["ws0"], ["ws0"],
               lambda e: e.tensor_scalar(PSF[:], PSF[:], 2560.0, None, ALU.is_le))
        def trvc(e):
            last = None
            for t in range(NT):
                last = e.transpose(PS[4][:, t * 8:(t + 1) * 8],
                                   PSF[0:8, t * 128:(t + 1) * 128],
                                   self.IDTF[0:8, 0:8])
            return last

        p.emit("pe", ["ws0", "IDTF"], ["pb4"], trvc)
        p.emit("dve", ["pb4"], ["VCT"],
               lambda e: e.tensor_copy(SMB[:, 896:1024], PS[4][:, 0:128]))
        p.emit("dve", ["CMB", "MSKT"], ["CMB"],
               lambda e: e.tensor_tensor(SM[:, 384:512], SM[:, 384:512],
                                         SMB[:, 768:896], ALU.mult))
        p.emit("dve", ["CMB", "VCT"], ["CMB"],
               lambda e: e.tensor_tensor(SM[:, 384:512], SM[:, 384:512],
                                         SMB[:, 896:1024], ALU.mult))
        p.emit("dve", ["CMB"], ["CMB"],
               lambda e: e.tensor_scalar(SM[:, 384:512], SM[:, 384:512], 0.5,
                                         None, ALU.mult))

    def _ffn1_b2(self, wts):
        p = self.p
        SM = self.SM
        PS = self.PS
        WFM = p.sb_at("WFM", [8, 2048], F32, self.W0)
        B2S = p.sb_at("B2S", [8, 1024], F32, self.W1)
        p.emit_dma("sp", [], ["ws1"], lambda e: [
            e.dma_start(out=B2S[:], in_=wts["b2"])], stream="io")

        for g in range(4):
            def trw(e, g=g):
                last = None
                for q in range(4):
                    t = g * 4 + q
                    last = e.transpose(PS[4][0:8, q * 128:(q + 1) * 128],
                                       SM[:, 384 + t * 8:384 + t * 8 + 8],
                                       self.IDTF[:])
                return last

            p.emit("pe", ["CMB", "IDTF"], ["pb4"], trw)
            p.emit("dve", ["pb4"], ["ws0"],
                   lambda e, g=g: e.tensor_copy(WFM[:, g * 512:(g + 1) * 512],
                                                PS[4][0:8, 0:512]))
        for t in range(NT):
            for dh in range(2):
                def bmm(e, t=t, dh=dh):
                    return e.matmul(PS[5][:], WFM[:, t * 128:(t + 1) * 128],
                                    B2S[:, dh * 512:(dh + 1) * 512],
                                    start=True, stop=True)

                p.emit("pe", ["ws0", "ws1"], ["pb5"], bmm)
                xt = self.X[:, t * D + dh * 512: t * D + dh * 512 + 512]

                def ev(e, xt=xt):
                    return e.tensor_tensor(xt, PS[5][:], xt, ALU.add)

                p.emit("dve", ["pb5", "X"], ["X"], ev)


    def attn1(self, wts, view_src, view_normed, xn_sink=None, post_sink=None, view_reads=()):
        p = self.p
        AA, AB, SM, SMB, SC = self.AA, self.AB, self.SM, self.SMB, self.SC
        PS = self.PS

        MQ = p.sb_at("MQ", [128, 4096], BF16, self.W0)
        MK = p.sb_at("MK", [128, 4096], BF16, self.W0, 8192)
        MV = p.sb_at("MV", [128, 8192], BF16, self.W1)
        XSTG = p.sb_at("XSTGa", [128, 1024], F32, self.SMB)
        p.emit_dma("pool", [], ["ws0"], lambda e: [
            e.dma_start(out=MQ[:].rearrange("p (c f) -> p c f", f=512),
                        in_=wts["MQ"].rearrange("(c p) f -> p c f", p=128)),
            e.dma_start(out=MK[:].rearrange("p (c f) -> p c f", f=512),
                        in_=wts["MK"].rearrange("(c p) f -> p c f", p=128)),
        ], n=2, stream="w0")
        p.emit_dma("pool", [], ["ws1"], lambda e: [
            e.dma_start(out=MV[:].rearrange("p (c f) -> p c f", f=1024),
                        in_=wts["MV"].rearrange("(c p) f -> p c f", p=128)),
        ], stream="w1")

        self.ln_stats_all()
        for t in range(NT):
            slot = t % 2
            sck = self.ln_apply(t, slot)
            srcsc = SC[:, slot * D:(slot + 1) * D]

            def tr(e, srcsc=srcsc):
                last = None
                for c in range(8):
                    last = e.transpose(self.PST[:, c * 128:(c + 1) * 128],
                                       srcsc[:, c * 128:(c + 1) * 128], self.IDT[:])
                return last

            if xn_sink is not None:
                p.emit_dma("sp", [sck], ["E2IN"], lambda e, t=t, sck=sck, srcsc=srcsc: [
                    e.dma_start(out=xn_sink(t), in_=srcsc)], stream="xo")
            p.emit("pe", [sck, "IDT"], ["PST"], tr)
            xf = SC[:, 0:D] if slot == 1 else SC[:, D:2 * D]
            xfk = "SCs0" if slot == 1 else "SCs1"
            p.emit("dve", ["PST"], [xfk],
                   lambda e, xf=xf: e.tensor_copy(
                       xf.rearrange("p (c f) -> p c f", f=128),
                       self.PST[:, 0:D].rearrange("p (c f) -> p c f", f=128)))

            for dst, W, pbk, bank in (("ABq", MQ, "pb2", 2), ("ABk", MK, "pb3", 3)):
                def qmm(e, xf=xf, W=W, bank=bank):
                    last = None
                    for qc in range(4):
                        for dc in range(8):
                            last = e.matmul(
                                PS[bank][:, qc * 128:(qc + 1) * 128],
                                W[:, dc * 512 + qc * 128:dc * 512 + qc * 128 + 128],
                                xf[:, dc * 128:(dc + 1) * 128],
                                start=(dc == 0), stop=(dc == 7))
                    return last

                p.emit("pe", [xfk, "ws0"], [pbk], qmm)
                off = 0 if dst == "ABq" else 8192
                p.emit("act", [pbk], [dst],
                       lambda e, t=t, off=off, bank=bank: e.copy(
                           AB[:, off:off + 8192].rearrange(
                               "p (c n) -> p c n", n=2048)[:, :, t * 128:(t + 1) * 128],
                           PS[bank][:, 0:512].rearrange("p (c n) -> p c n", n=128)))

        if post_sink is not None:
            post_sink()
        for v in range(32):
            if not view_normed:
                p.emit_dma("sp", list(view_reads), ["SMB"], lambda e, v=v: [
                    e.dma_start(out=XSTG[:], in_=view_src(v))], stream="xs")
                self.ln_tile_seq(XSTG[:], "SMB", SC[:, D:2 * D], "SCs1",
                                 scol=(v % 2) * 24)
            else:
                p.emit_dma("sp", list(view_reads), ["SCs1"], lambda e, v=v: [
                    e.dma_start(out=SC[:, D:2 * D], in_=view_src(v))], stream="xs")

            def trv(e):
                last = None
                for c in range(8):
                    last = e.transpose(self.PST[:, c * 128:(c + 1) * 128],
                                       SC[:, D + c * 128:D + (c + 1) * 128],
                                       self.IDT[:])
                return last

            p.emit("pe", ["SCs1", "IDT"], ["PST"], trv)
            p.emit("dve", ["PST"], ["SCs0"],
                   lambda e: e.tensor_copy(
                       SC[:, 0:D].rearrange("p (c f) -> p c f", f=128),
                       self.PST[:, 0:D].rearrange("p (c f) -> p c f", f=128)))
            xf = SC[:, 0:D]
            for vh in range(2):
                def vmm1(e, xf=xf, vh=vh):
                    last = None
                    for dc in range(8):
                        last = e.matmul(PS[6][:],
                                        xf[:, dc * 128:(dc + 1) * 128],
                                        MV[:, dc * 1024 + vh * 512:dc * 1024 + vh * 512 + 512],
                                        start=(dc == 0), stop=(dc == 7))
                    return last

                p.emit("pe", ["SCs0", "ws1"], ["pb6"], vmm1)
                p.emit("act", ["pb6"], ["AA"],
                       lambda e, v=v, vh=vh: e.copy(
                           AA[:, v * 1024 + vh * 512: v * 1024 + vh * 512 + 512],
                           PS[6][:]))

        QH = p.sb_at("QH", [128, 4096], BF16, self.W0)
        KH = p.sb_at("KH", [128, 4096], BF16, self.W0, 8192)

        def kslab(cb, tok0, n):
            return KH[0:64, cb * 512 + tok0: cb * 512 + tok0 + n]

        def qslab(qb, tok0, n):
            return QH[0:64, qb * 512 + tok0: qb * 512 + tok0 + n]

        MX8 = SM[:, 0:32:4]
        DP8 = SM[:, 64:96:4]
        MM = SM[:, 128:129]
        NB = SM[:, 132:133]
        DD = SM[:, 136:137]
        LD = SM[:, 140:141]
        for hl in range(4):
            t0 = hl * 512
            for (srck, dsth, off) in (("ABq", QH, 0), ("ABk", KH, 8192)):
                for sb_i in range(8):
                    ch, base = sb_i // 2, (sb_i % 2) * 64

                    def exmm(e, ch=ch, base=base, off=off, t0=t0):
                        return e.matmul(PS[6][0:64, :],
                                        self.IDT[:, base:base + 64],
                                        AB[:, off + ch * 2048 + t0:
                                           off + ch * 2048 + t0 + 512],
                                        start=True, stop=True)

                    p.emit("pe", [srck, "IDT"], ["pb6"], exmm)
                    p.emit("dve", ["pb6"], ["ws0"],
                           lambda e, dsth=dsth, sb_i=sb_i: e.tensor_copy(
                               dsth[0:64, sb_i * 512:(sb_i + 1) * 512],
                               PS[6][0:64, :]))
            for ci in range(32):
                cb, ctr = ci // 4, (ci % 4) * 128
                for qb in range(8):
                    bank = 4 + (qb % 2)

                    def smm(e, cb=cb, ctr=ctr, qb=qb, bank=bank):
                        return e.matmul(PS[bank][:], kslab(cb, ctr, 128),
                                        qslab(qb, 0, 512), start=True, stop=True)

                    p.emit("pe", ["ws0"], [f"pb{bank}"], smm)
                    p.emit("dve", [f"pb{bank}"], ["LNS"],
                           lambda e, qb=qb, bank=bank: e.reduce_max(
                               SM[:, qb * 4:qb * 4 + 1], PS[bank][:], axis=AX),
                           strict=True)
                p.emit("dve", ["LNS"], ["LNS"],
                       lambda e: e.reduce_max(MM, MX8, axis=AX), strict=True)
                p.emit("dve", ["LNS"], ["LNS"],
                       lambda e: e.tensor_scalar(NB, MM, -0.125, None, ALU.mult),
                       strict=True)
                for qb in range(8):
                    bank = 4 + (qb % 2)

                    def smm2(e, cb=cb, ctr=ctr, qb=qb, bank=bank):
                        return e.matmul(PS[bank][:], kslab(cb, ctr, 128),
                                        qslab(qb, 0, 512), start=True, stop=True)

                    p.emit("pe", ["ws0"], [f"pb{bank}"], smm2)
                    p.emit("act", [f"pb{bank}", "LNS", "CONSTS"], ["LNS", "SCs0"],
                           lambda e, qb=qb, bank=bank: e.activation(
                               SC[:, 0:512], PS[bank][:], AF.Exp, bias=NB,
                               scale=0.125,
                               accum_out=SM[:, 64 + qb * 4:64 + qb * 4 + 1]),
                           strict=True)
                p.emit("dve", ["LNS"], ["LNS"],
                       lambda e: e.reduce_sum(DD, DP8, axis=AX), strict=True)
                p.emit("act", ["LNS"], ["LNS"],
                       lambda e: e.activation(LD, DD, AF.Ln, bias=0.0, scale=1.0),
                       strict=True)
                p.emit("dve", ["LNS"], ["CMB"],
                       lambda e, ci=ci: e.scalar_tensor_tensor(
                           SM[:, 384 + ci * 4:384 + ci * 4 + 1], LD, -1.0, NB,
                           ALU.mult, ALU.add), strict=True)
            for lt in range(4):
                tloc = hl * 4 + lt
                for ci in range(32):
                    cb, ctr = ci // 4, (ci % 4) * 128
                    for qb in range(8):
                        bank = 4 + (qb % 2)

                        def smm3(e, cb=cb, ctr=ctr, qb=qb, lt=lt, bank=bank):
                            return e.matmul(PS[bank][:, 0:128],
                                            kslab(cb, ctr, 128),
                                            qslab(qb, lt * 128, 128),
                                            start=True, stop=True)

                        p.emit("pe", ["ws0"], [f"pb{bank}"], smm3)
                        p.emit("act", [f"pb{bank}", "CMB", "CONSTS"], ["SCs0"],
                               lambda e, ci=ci, qb=qb, bank=bank: e.activation(
                                   SC[:, qb * 128:(qb + 1) * 128],
                                   PS[bank][:, 0:128], AF.Exp,
                                   bias=SM[:, 384 + ci * 4:384 + ci * 4 + 1],
                                   scale=0.125))
                    def avmm(e, ci=ci, cb=cb, lt=lt):
                        last = None
                        for h2 in range(8):
                            pbank = h2 // 4
                            last = e.matmul(
                                PS[pbank][:, (h2 % 4) * 128:(h2 % 4) * 128 + 128],
                                SC[:, h2 * 128:(h2 + 1) * 128],
                                AA[:, (h2 * 4 + ci % 4) * 1024 + cb * 128:
                                   (h2 * 4 + ci % 4) * 1024 + cb * 128 + 128],
                                start=(ci == 0 and h2 % 4 == 0),
                                stop=(ci == 31 and h2 % 4 == 3))
                        return last

                    p.emit("pe", ["SCs0", "AA"], ["pb0", "pb1"], avmm)
                for pbank in range(2):
                    xt = self.X[:, tloc * D + pbank * 512: tloc * D + pbank * 512 + 512]

                    def ev(e, xt=xt, pbank=pbank):
                        return e.tensor_tensor(xt, PS[pbank][:], xt, ALU.add)

                    p.emit("dve", [f"pb{pbank}", "X"], ["X"], ev)

def _host_consts():
    cv = np.zeros((128, 16), dtype=np.float32)
    for i, v in enumerate(CONST_VALS):
        cv[:, i] = v
    return {
        "ident": np.eye(128, dtype=ml_dtypes.bfloat16),
        "identf": np.eye(128, dtype=np.float32),
        "ones": np.ones((1, 128), dtype=ml_dtypes.bfloat16),
        "constv": cv,
    }


def build_ffn0_launch(debug=False):
    p = Prog()
    mk = MK(p)
    x_in = p.din("x_in", [NTOK, D], F32)
    ident = p.din("ident", [128, 128], BF16)
    identf = p.din("identf", [128, 128], F32)
    ones = p.din("ones", [1, 128], BF16)
    constv = p.din("constv", [128, 16], F32)
    gate_w = p.din("gate_w", [D, 8], F32)
    w1 = p.din("w1", [8, D, 2048], F32)
    w2 = p.din("w2", [8, 2048, D], F32)
    x_out = p.dout("x_out", [NTOK, D], F32)
    if debug:
        dbg_ab = p.dout("dbg_ab", [128, 16384], BF16)
        dbg_cmb = p.dout("dbg_cmb", [128, 128], F32)
        dbg_h = p.dout("dbg_h", [128, 32768], BF16)
    mk.load_const(ident, identf, ones, constv)
    mk.load_x(x_in)
    mk.ffn(0, {"gate_w": gate_w, "w1": w1, "w2": w2},
           debug=({"ab": dbg_ab, "cmb": dbg_cmb, "h": dbg_h} if debug else None))
    mk.store_x(x_out)
    mk.finish()
    return p


def _mamba_consts(core):
    half = core % 2
    dead = np.zeros((8, 4096), np.float32)
    if half == 0:
        dead[:, :2048] = -1e30
    hsel = np.zeros((8, 1024), np.float32)
    for h in range(8):
        hsel[h, h * 128:(h + 1) * 128] = 1.0
    tri = np.tril(np.ones((128, 128), np.float32)).T
    tri = np.ascontiguousarray(np.tril(np.ones((128, 128), np.float32), 0).T)
    return dead, hsel, tri.astype(ml_dtypes.bfloat16)


def build_mamba_launch():
    p = Prog()
    mk = MK(p)
    x_in = p.din("x_in", [NTOK, D], F32)
    x_pr = p.din("x_pr", [NTOK, D], F32)
    ident = p.din("ident", [128, 128], BF16)
    identf = p.din("identf", [128, 128], F32)
    ones = p.din("ones", [1, 128], BF16)
    constv = p.din("constv", [128, 16], F32)
    wxt = p.din("wxt", [D, 136], F32)
    dtb = p.din("dt_bias", [8, 1], F32)
    alog = p.din("A_log", [8, 1], F32)
    dfull = p.din("dfull", [128, D], F32)
    deadv = p.din("deadv", [8, 4096], F32)
    hsel = p.din("hsel", [8, 1024], F32)
    tri = p.din("tri", [128, 128], BF16)
    x_out = p.dout("x_out", [NTOK, D], F32)
    mk.load_const(ident, identf, ones, constv)
    mk.load_x(x_in)
    xpr3 = x_pr.rearrange("(v p) d -> v p d", p=128)
    mk.mamba({"WXT": wxt, "dt_bias": dtb, "A_log": alog, "DFULL": dfull,
              "DEADV": deadv, "HSEL": hsel, "TRI": tri},
             lambda v: xpr3[v])
    mk.store_x(x_out)
    mk.finish()
    return p


def build_ffn1_launch(tag="a"):
    p = Prog()
    mk = MK(p)
    x_in = p.din("x_in", [NTOK, D], F32)
    ident = p.din("ident", [128, 128], BF16)
    identf = p.din("identf", [128, 128], F32)
    ones = p.din("ones", [1, 128], BF16)
    constv = p.din("constv", [128, 16], F32)
    router_w = p.din("router_w", [D, 8], F32)
    router_b = p.din("router_b", [8, 1], F32)
    b1 = p.din("b1", [8, 4096], F32)
    b2 = p.din("b2", [8, 1024], F32)
    w1 = p.din("w1", [8, D, 4096], F32)
    w2 = p.din("w2", [8, 4096, D], F32)
    prefmask = p.din("prefmask", [8, 1], F32)
    x_out = p.dout("x_out", [NTOK, D], F32)
    mk.load_const(ident, identf, ones, constv)
    mk.load_x(x_in)
    mk.ffn(1, {"router_w": router_w, "router_b": router_b, "b1": b1, "b2": b2,
               "w1": w1, "w2": w2, "prefmask": prefmask, "tag": tag})
    mk.store_x(x_out)
    mk.finish()
    return p


def build_attn1_launch():
    p = Prog()
    mk = MK(p)
    x_in = p.din("x_in", [NTOK, D], F32)
    x_seq = p.din("x_seq", [SEQ, D], F32)
    ident = p.din("ident", [128, 128], BF16)
    identf = p.din("identf", [128, 128], F32)
    ones = p.din("ones", [1, 128], BF16)
    constv = p.din("constv", [128, 16], F32)
    mq = p.din("MQ", [D, 512], F32)
    mk_in = p.din("MK", [D, 512], F32)
    mv = p.din("MV", [D, 1024], F32)
    x_out = p.dout("x_out", [NTOK, D], F32)
    mk.load_const(ident, identf, ones, constv)
    mk.load_x(x_in)
    xs3 = x_seq.rearrange("(v p) d -> v p d", p=128)
    mk.attn1({"MQ": mq, "MK": mk_in, "MV": mv}, lambda v: xs3[v], view_normed=False)
    mk.store_x(x_out)
    mk.finish()
    return p


def build_fused(upto=6):
    p = Prog()
    mk = MK(p)
    x_in = p.din("x_in", [NTOK, D], F32)
    ident = p.din("ident", [128, 128], BF16)
    identf = p.din("identf", [128, 128], F32)
    ones = p.din("ones", [1, 128], BF16)
    constv = p.din("constv", [128, 16], F32)
    gate_w = p.din("gate_w", [D, 8], F32)
    w1_0 = p.din("w1_0", [8, D, 2048], F32)
    w2_0 = p.din("w2_0", [8, 2048, D], F32)
    wxt = p.din("wxt", [D, 136], F32)
    dtb = p.din("dt_bias", [8, 1], F32)
    alog = p.din("A_log", [8, 1], F32)
    dfull = p.din("dfull", [128, D], F32)
    deadv = p.din("deadv", [8, 4096], F32)
    hsel = p.din("hsel", [8, 1024], F32)
    tri = p.din("tri", [128, 128], BF16)
    router_w = p.din("router_w", [D, 8], F32)
    router_b = p.din("router_b", [8, 1], F32)
    b1 = p.din("b1", [8, 4096], F32)
    b2 = p.din("b2", [8, 1024], F32)
    w1_1 = p.din("w1_1", [8, D, 4096], F32)
    w2_1 = p.din("w2_1", [8, 4096, D], F32)
    prefmask = p.din("prefmask", [8, 1], F32)
    mq = p.din("MQ", [D, 512], F32)
    mk_in = p.din("MK", [D, 512], F32)
    mv = p.din("MV", [D, 1024], F32)
    x_out = p.dout("x_out", [NTOK, D], F32)

    PAIRS = [[0, 1], [2, 3], [4, 5], [6, 7]]
    e1_in = p.dtmp("e1_in", [NTOK, D], F32)
    e1_out = p.dtmp("e1_out", [SEQ, D], F32)
    e2_in = p.dtmp("e2_in", [NTOK, D], BF16)
    e2_out = p.dtmp("e2_out", [SEQ, D], BF16)

    mk.load_const(ident, identf, ones, constv)
    mk.load_x(x_in)

    if upto >= 1:
        mk.ffn(0, {"gate_w": gate_w, "w1": w1_0, "w2": w2_0})

    p.emit_dma("sp", ["X"], ["E1IN"], lambda e: [
        e.dma_start(out=e1_in.rearrange("(t p) d -> p t d", p=128),
                    in_=mk.X[:].rearrange("p (t d) -> p t d", d=D))], stream="xo")
    p.emit_cc(["E1IN"], ["E1OUT"], lambda e: e.collective_compute(
        "AllGather", ALU.bypass, replica_groups=PAIRS,
        ins=[e1_in.tensor.ap().opt()], outs=[e1_out.tensor.ap().opt()]))
    e1v = e1_out.rearrange("(v p) d -> v p d", p=128)

    def mamba_partner(v):
        return e1v[v]

    if upto >= 2:
        mk.mamba({"WXT": wxt, "dt_bias": dtb, "A_log": alog, "DFULL": dfull,
                  "DEADV": deadv, "HSEL": hsel, "TRI": tri},
                 mamba_partner, partner_bf16=False, partner_reads=("E1OUT",))

    if upto >= 3:
        mk.ffn(0, {"gate_w": gate_w, "w1": w1_0, "w2": w2_0})

    if upto >= 4:
        mk.ffn(1, {"router_w": router_w, "router_b": router_b, "b1": b1, "b2": b2,
                   "w1": w1_1, "w2": w2_1, "prefmask": prefmask, "tag": "a"})

    e2i = e2_in.rearrange("(t p) d -> t p d", p=128)
    e2v = e2_out.rearrange("(v p) d -> v p d", p=128)

    def post_sink():
        p.emit_cc(["E2IN"], ["E2OUT"], lambda e: e.collective_compute(
            "AllGather", ALU.bypass, replica_groups=PAIRS,
            ins=[e2_in.tensor.ap().opt()], outs=[e2_out.tensor.ap().opt()]))

    if upto >= 5:
        mk.attn1({"MQ": mq, "MK": mk_in, "MV": mv},
                 lambda v: e2v[v], view_normed=True,
                 xn_sink=lambda t: e2i[t], post_sink=post_sink,
                 view_reads=("E2OUT",))

    if upto >= 6:
        mk.ffn(1, {"router_w": router_w, "router_b": router_b, "b1": b1, "b2": b2,
                   "w1": w1_1, "w2": w2_1, "prefmask": prefmask, "tag": "b"})

    mk.store_x(x_out)
    mk.finish()
    return p


def kernel(**inputs):
    consts = _host_consts()
    f32 = np.float32
    inp = {k: np.ascontiguousarray(np.asarray(v, f32)) for k, v in inputs.items()}
    x = np.ascontiguousarray(inp["L0_attn_x"].reshape(16384, D))
    p, nc = _get_prog("fused", build_fused)
    dfull = np.ascontiguousarray(
        np.broadcast_to(np.repeat(inp["L0_attn_D"], 128)[None, :], (128, D)))
    wxt = np.ascontiguousarray(inp["L0_attn_W_xproj"].T)
    in_maps = []
    for c in range(NCORES):
        dead, hselv, triv = _mamba_consts(c)
        pref = np.zeros((8, 1), f32)
        pref[:c] = 1.0
        in_maps.append({
            "x_in": x[c * NTOK:(c + 1) * NTOK], **consts,
            "gate_w": inp["L0_ffn_gate_w"], "w1_0": inp["L0_ffn_w1"],
            "w2_0": inp["L0_ffn_w2"],
            "wxt": wxt, "dt_bias": inp["L0_attn_dt_bias"].reshape(8, 1),
            "A_log": inp["L0_attn_A_log"].reshape(8, 1), "dfull": dfull,
            "deadv": dead, "hsel": hselv, "tri": triv,
            "router_w": inp["L1_ffn_router_w"],
            "router_b": inp["L1_ffn_router_b"].reshape(8, 1),
            "b1": inp["L1_ffn_b1"], "b2": inp["L1_ffn_b2"],
            "w1_1": inp["L1_ffn_w1"], "w2_1": inp["L1_ffn_w2"],
            "prefmask": pref,
            "MQ": inp["L1_attn_Mq"], "MK": inp["L1_attn_Mk"],
            "MV": inp["L1_attn_Mv"],
        })
    res = run_bass_kernel_spmd(nc, in_maps, core_ids=list(range(NCORES)))
    out = np.concatenate([res.results[c]["x_out"] for c in range(NCORES)], 0)
    return np.ascontiguousarray(out).reshape(4, SEQ, D)


_PROG_CACHE = {}
_PROG_CACHE = {}


def _get_prog(name, builder):
    if name not in _PROG_CACHE:
        p = builder()
        _PROG_CACHE[name] = (p, p.build())
    return _PROG_CACHE[name]


# version 06
# speedup vs baseline: 1.0666x; 1.0666x over previous
import numpy as np
import ml_dtypes
from contextlib import ExitStack

import concourse.bass as bass
import concourse.mybir as mybir
from concourse.bass_utils import run_bass_kernel_spmd

F32 = mybir.dt.float32
BF16 = mybir.dt.bfloat16
AF = mybir.ActivationFunctionType
ALU = mybir.AluOpType
AX = mybir.AxisListType.X

D = 1024
NTOK = 2048
NT = NTOK // 128
NCORES = 8
SEQ = 4096

BLOCK_ATTR = {"pe": "tensor", "act": "scalar", "dve": "vector", "pool": "gpsimd", "sp": "sync"}
CONST_VALS = [1e-5, -0.5, -1.0, 0.5, 0.125, 2.0, -1e30]


def _dt_size(dt):
    return {F32: 4, BF16: 2}[dt]


class Prog:

    def __init__(self, name="mk"):
        self.nc = bass.Bass("TRN2", target_bir_lowering=False,
                            detect_race_conditions=False)
        self.queues = {e: [] for e in BLOCK_ATTR}
        self.sem_names = ["pe", "act", "dve", "pool", "sp"]
        self.count = {s: 0 for s in self.sem_names}
        self.hwm = {e: {s: 0 for s in self.sem_names} for e in BLOCK_ATTR}
        self.writer = {}
        self.readers = {}
        self.sem_h = {}
        self._sb_off = 16512
        self.inputs = {}
        self.outputs = {}

    def sb(self, name, shape, dtype):
        off = (self._sb_off + 31) // 32 * 32
        h = self.nc.alloc_sbuf_tensor_at(name, list(shape), dtype, offset=off)
        self._sb_off = off + int(np.prod(shape[1:])) * _dt_size(dtype)
        assert self._sb_off <= 229344, f"SBUF overflow at {name}: {self._sb_off}"
        return h

    def sb_at(self, name, shape, dtype, base, byte_off=0):
        off = base.manual_sbuf_range[0] + byte_off
        assert off + int(np.prod(shape[1:])) * _dt_size(dtype) <= base.manual_sbuf_range[1] or True
        return self.nc.alloc_sbuf_tensor_at(name, list(shape), dtype, offset=off)

    def ps(self, name, shape, dtype=F32):
        return self.nc.alloc_psum_tensor(name, list(shape), dtype)

    def din(self, name, shape, dtype=F32):
        ap = self.nc.dram_tensor(name, list(shape), dtype, kind="ExternalInput").ap()
        self.inputs[name] = (shape, dtype)
        return ap

    def dout(self, name, shape, dtype=F32):
        ap = self.nc.dram_tensor(name, list(shape), dtype, kind="ExternalOutput").ap()
        self.outputs[name] = (shape, dtype)
        return ap

    def dtmp(self, name, shape, dtype=F32):
        return self.nc.dram_tensor(name, list(shape), dtype, kind="Internal").ap()

    def _dep_events(self, eng, reads, writes):
        need = {}

        def add(ev):
            if ev is None:
                return
            sem, val, oeng = ev
            if oeng == eng:
                return
            if val > self.hwm[eng][sem]:
                need[sem] = max(need.get(sem, 0), val)

        for k in reads:
            add(self.writer.get(k))
        for k in writes:
            add(self.writer.get(k))
            for (sem, oeng), val in self.readers.get(k, {}).items():
                add((sem, val, oeng))
        for sem, val in need.items():
            self.hwm[eng][sem] = val
        return list(need.items())

    def _record(self, reads, writes, ev):
        sem, val, oeng = ev
        for k in writes:
            self.writer[k] = ev
            self.readers[k] = {}
        for k in reads:
            d = self.readers.setdefault(k, {})
            kk = (sem, oeng)
            d[kk] = max(d.get(kk, 0), val)

    def emit(self, eng, reads, writes, body, strict=False):
        waits = self._dep_events(eng, reads, writes)
        self.count[eng] += 1
        val = self.count[eng]

        def run(e, waits=waits, body=body, val=val, sem=eng):
            for s, v in waits:
                e.wait_ge(self.sem_h[s], v)
            last = body(e)
            assert last is not None, "group body must return its last instruction"
            last.then_inc(self.sem_h[sem], 1)

        self.queues[eng].append(run)
        self._record(reads, writes, (eng, val, None if strict else eng))
        return (eng, val, eng)

    def _stream_sem(self, stream, kind):
        sem = f"{kind}_{stream}"
        if sem not in self.count:
            self.sem_names.append(sem)
            self.count[sem] = 0
            for e in BLOCK_ATTR:
                self.hwm[e][sem] = 0
        return sem

    def emit_dma(self, eng, reads, writes, body, n=1, stream="io"):
        sem = self._stream_sem(stream, "dq")
        waits = self._dep_events(eng, reads, writes)
        prev = self.count[sem]
        if prev > self.hwm[eng][sem]:
            waits = waits + [(sem, prev)]
            self.hwm[eng][sem] = prev
        self.count[sem] = prev + 16 * n
        val = self.count[sem]

        def run(e, waits=waits, body=body, n=n, sem=sem):
            for s, v in waits:
                e.wait_ge(self.sem_h[s], v)
            insts = body(e)
            assert len(insts) == n
            for ins in insts:
                ins.then_inc(self.sem_h[sem], 16)

        self.queues[eng].append(run)
        self._record(reads, writes, (sem, val, None))
        return (sem, val, None)

    def emit_cc(self, reads, writes, body, stream="cc"):
        sem = self._stream_sem(stream, "cs")
        waits = self._dep_events("pool", reads, writes)
        prev = self.count[sem]
        if prev > self.hwm["pool"][sem]:
            waits = waits + [(sem, prev)]
            self.hwm["pool"][sem] = prev
        self.count[sem] = prev + 1
        val = self.count[sem]

        def run(e, waits=waits, body=body, sem=sem):
            for s, v in waits:
                e.wait_ge(self.sem_h[s], v)
            body(e).then_inc(self.sem_h[sem], 1)

        self.queues["pool"].append(run)
        self._record(reads, writes, (sem, val, None))
        return (sem, val, None)

    def build(self):
        with ExitStack() as ctx:
            for s in self.sem_names:
                self.sem_h[s] = ctx.enter_context(self.nc.semaphore(s))
            block = ctx.enter_context(self.nc.Block())
            for eng, attr in BLOCK_ATTR.items():
                items = self.queues[eng]
                if not items:
                    continue

                def runner(e, items=items):
                    for f in items:
                        f(e)

                getattr(block, attr)(runner)
        return self.nc



class MK:

    def __init__(self, p: Prog):
        self.p = p
        self.X = p.sb("X", [128, NT * D], F32)
        self.AA = p.sb("AA", [128, 32768], BF16)
        self.AB = p.sb("AB", [128, 16384], BF16)
        self.W0 = p.sb("W0", [128, 8192], BF16)
        self.W1 = p.sb("W1", [128, 8192], BF16)
        self.SC = p.sb("SC", [128, 2048], BF16)
        self.SM = p.sb("SM", [128, 1024], F32)
        self.SMB = p.sb("SMB", [128, 1024], F32)
        self.GWT = p.sb("GWT", [128, 64], BF16)
        self.IDT = p.sb("IDT", [128, 128], BF16)
        self.IDTF = p.sb("IDTF", [128, 128], F32)
        self.ONE1 = p.sb("ONE1", [1, 128], BF16)
        self.CONSTS = p.sb("CONSTS", [128, 16], F32)
        for i, v in enumerate(CONST_VALS):
            p.nc.const_aps.register(F32, v, self.CONSTS[:, i:i + 1])
        self.PS = [p.ps(f"ps{i}", [128, 512], F32) for i in range(7)]
        self.PST = p.ps("pst", [128, 1024], BF16)
        self.wslots = [self.W0, self.W1]


    def load_const(self, ident_in, identf_in, ones_in, constv_in):
        p = self.p
        p.emit_dma("sp", [], ["IDT", "IDTF", "ONE1", "CONSTS"], lambda e: [
            e.dma_start(out=self.IDT[:], in_=ident_in),
            e.dma_start(out=self.IDTF[:], in_=identf_in),
            e.dma_start(out=self.ONE1[:], in_=ones_in),
            e.dma_start(out=self.CONSTS[:], in_=constv_in),
        ], n=4, stream="io")

    def load_x(self, x_in):
        self.p.emit_dma("sp", [], ["X"], lambda e: [
            e.dma_start(out=self.X[:].rearrange("p (t d) -> p t d", d=D),
                        in_=x_in.rearrange("(t p) d -> p t d", p=128))], stream="io")

    def store_x(self, x_out):
        self.p.emit_dma("sp", ["X"], ["XOUT"], lambda e: [
            e.dma_start(out=x_out.rearrange("(t p) d -> p t d", p=128),
                        in_=self.X[:].rearrange("p (t d) -> p t d", d=D))], stream="io")

    def finish(self):
        p = self.p
        waits = [(s, p.count[s]) for s in p.sem_names
                 if (s.startswith("dq_") or s.startswith("cs_")) and p.count[s]]

        def run(e, waits=waits):
            for s, v in waits:
                e.wait_ge(p.sem_h[s], v)

        p.queues["sp"].append(run)

    def ln_stats_all(self, nt=NT, x_of=None, x_key="X"):
        p = self.p
        SM = self.SM
        for t in range(nt):
            xt = self.X[:, t * D:(t + 1) * D] if x_of is None else x_of(t)
            s1 = SM[:, 4 * t:4 * t + 1]
            s2 = SM[:, 64 + 4 * t:64 + 4 * t + 1]
            sc = self.SC[:, (t % 2) * D:(t % 2) * D + D]
            p.emit("dve", [x_key], ["LNS"],
                   lambda e, s1=s1, xt=xt: e.reduce_sum(s1, xt, axis=AX), strict=True)
            p.emit("act", [x_key], ["LNS", f"SCs{t % 2}"],
                   lambda e, s2=s2, xt=xt, sc=sc: e.activation(
                       sc, xt, AF.Square, bias=0.0, scale=1.0, accum_out=s2),
                   strict=True)
        s1v = SM[:, 0:4 * nt:4]
        s2v = SM[:, 64:64 + 4 * nt:4]
        nmv = SM[:, 128:128 + 4 * nt:4]
        tmpv = SM[:, 192:192 + 4 * nt:4]
        varv = SM[:, 256:256 + 4 * nt:4]
        rsv = SM[:, 320:320 + 4 * nt:4]
        p.emit("dve", ["LNS"], ["LNS"],
               lambda e: e.tensor_scalar(nmv, s1v, -1.0 / D, None, ALU.mult),
               strict=True)
        p.emit("act", ["LNS"], ["LNS"],
               lambda e: e.activation(tmpv, nmv, AF.Square, bias=0.0, scale=1.0),
               strict=True)
        p.emit("dve", ["LNS"], ["LNS"],
               lambda e: e.scalar_tensor_tensor(varv, s2v, 1.0 / D, tmpv,
                                                ALU.mult, ALU.subtract), strict=True)
        p.emit("act", ["LNS", "CONSTS"], ["LNS"],
               lambda e: e.activation(rsv, varv, AF.Ln, bias=1e-5, scale=1.0),
               strict=True)
        p.emit("act", ["LNS", "CONSTS"], ["LNS"],
               lambda e: e.activation(rsv, rsv, AF.Exp, bias=0.0, scale=-0.5),
               strict=True)

    def ln_apply(self, t, out_slot, keep_f32=None, x_ap=None, x_key="X"):
        p = self.p
        SM = self.SM
        xt = self.X[:, t * D:(t + 1) * D] if x_ap is None else x_ap
        sck = f"SCs{out_slot}"
        sc = self.SC[:, out_slot * D:(out_slot + 1) * D]
        nmean = SM[:, 128 + 4 * t:128 + 4 * t + 1]
        rstd = SM[:, 320 + 4 * t:320 + 4 * t + 1]
        outs = [sck] + ([keep_f32[1]] if keep_f32 else [])

        def apply(e):
            last = e.tensor_scalar(sc, xt, nmean, rstd, ALU.add, ALU.mult)
            if keep_f32 is not None:
                last = e.tensor_scalar(keep_f32[0], xt, nmean, rstd, ALU.add, ALU.mult)
            return last

        p.emit("dve", [x_key, "LNS"], outs, apply)
        return sck

    def transpose_tile(self, src_ap, src_key, dst_ap3, dst_key, nchunk=8):
        p = self.p

        def tr(e):
            last = None
            for c in range(nchunk):
                last = e.transpose(self.PST[:, c * 128:(c + 1) * 128],
                                   src_ap[:, c * 128:(c + 1) * 128], self.IDT[:])
            return last

        p.emit("pe", [src_key, "IDT"], ["PST"], tr)

        def cp(e):
            return e.tensor_copy(dst_ap3, self.PST[:, 0:nchunk * 128].rearrange(
                "p (c f) -> p c f", f=128))

        p.emit("dve", ["PST"], [dst_key], cp)

    def ffn(self, layer, wts, debug=None):
        p = self.p
        FF = 2048 if layer == 0 else 4096
        NFC = FF // 128

        self.ln_stats_all()
        for t in range(NT):
            slot = t % 2
            sck = self.ln_apply(t, slot)
            dst = self.AB[:].rearrange("p (c tk) -> p c tk", tk=NTOK)[
                :, :, t * 128:(t + 1) * 128]
            self.transpose_tile(self.SC[:, slot * D:(slot + 1) * D], sck, dst, "AB")

        ab_fm = lambda dc, c0, c1: self.AB[:, dc * NTOK + c0: dc * NTOK + c1]

        if layer == 0:
            self._router0(wts)
        else:
            self._router1(wts)

        if debug is not None:
            p.emit_dma("sp", ["AB"], ["DBGAB"], lambda e: [
                e.dma_start(out=debug["ab"], in_=self.AB[:])], stream="io")
            p.emit_dma("sp", ["CMB"], ["DBGC"], lambda e: [
                e.dma_start(out=debug["cmb"], in_=self.SM[:, 384:512])], stream="io")

        FGW = 1024
        NFG = FF // FGW
        THALF = [(0, NT)] if layer == 0 else [(0, NT // 2), (NT // 2, NT)]
        w1ap, w2ap = wts["w1"], wts["w2"]

        NW2 = 2 if layer == 0 else 4
        W2W = 1024 // NW2
        plan = []
        for ei in range(8):
            for th in range(len(THALF)):
                for fg in range(NFG):
                    plan.append(("w1", ei, fg))
                for q in range(NW2):
                    plan.append(("w2", ei, q))

        li = [0]

        def emit_load(i):
            kind, ei, idx = plan[i]
            s = i % 2
            sl = self.wslots[s]
            if kind == "w1":
                src = w1ap[ei, :, idx * FGW:(idx + 1) * FGW].rearrange(
                    "(c p) f -> p c f", p=128)
                p.emit_dma("pool", [], [f"ws{s}"], lambda e: [
                    e.dma_start(out=sl[:].rearrange("p (c f) -> p c f", f=FGW),
                                in_=src)], stream=f"w{s}")
            else:
                src = w2ap[ei, :, idx * W2W:(idx + 1) * W2W].rearrange(
                    "(c p) d -> p c d", p=128)
                p.emit_dma("pool", [], [f"ws{s}"], lambda e: [
                    e.dma_start(out=sl[:, 0:NFC * W2W].rearrange(
                        "p (c d) -> p c d", d=W2W), in_=src)], stream=f"w{s}")

        def ensure_loads(upto):
            while li[0] <= min(upto, len(plan) - 1):
                emit_load(li[0])
                li[0] += 1

        ensure_loads(1)
        ci = 0
        hb = [0]
        for ei in range(8):
            for th_i, (t0, t1) in enumerate(THALF):
                ntile = t1 - t0
                TW = ntile * 128
                for fg in range(NFG):
                    s = ci % 2
                    sl = self.wslots[s]
                    for fc_in in range(FGW // 128):
                        fc = fg * (FGW // 128) + fc_in
                        for st in range(ntile // 4):
                            c0 = (t0 + st * 4) * 128
                            bank = hb[0] % 2
                            hb[0] += 1
                            psk = f"pb{bank}"

                            def mm(e, sl=sl, fc_in=fc_in, c0=c0, bank=bank):
                                last = None
                                for dc in range(8):
                                    last = e.matmul(
                                        self.PS[bank][:],
                                        sl[:, dc * FGW + fc_in * 128:
                                           dc * FGW + fc_in * 128 + 128],
                                        ab_fm(dc, c0, c0 + 512),
                                        start=(dc == 0), stop=(dc == 7))
                                return last

                            p.emit("pe", [f"ws{s}", "AB"], [psk], mm)

                            col = fc * TW + (c0 - t0 * 128)
                            dst = self.AA[:, col:col + 512]
                            if layer == 0:
                                def ev(e, dst=dst, bank=bank):
                                    return e.activation(dst, self.PS[bank][:],
                                                        AF.Silu, bias=0.0, scale=1.0)
                                p.emit("act", [psk], ["AA"], ev)
                            else:
                                bcol = 512 + fc * 8 + ei

                                def ev(e, dst=dst, bank=bank, bcol=bcol):
                                    return e.activation(
                                        dst, self.PS[bank][:], AF.Gelu,
                                        bias=self.SM[:, bcol:bcol + 1], scale=1.0)
                                p.emit("act", [psk, "B1T"], ["AA"], ev)
                    ci += 1
                    ensure_loads(ci + 1)
                if debug is not None and ei == 0:
                    p.emit_dma("sp", ["AA"], ["DBGH"], lambda e: [
                        e.dma_start(out=debug["h"], in_=self.AA[:])], stream="io")
                for q in range(NW2):
                    s = ci % 2
                    sl = self.wslots[s]
                    for t in range(t0, t1):
                        bank = 2 + (t % 2)
                        psk = f"pb{bank}"
                        tloc = t - t0

                        def ymm(e, sl=sl, tloc=tloc, bank=bank, TW=TW):
                            last = None
                            for fc in range(NFC):
                                last = e.matmul(
                                    self.PS[bank][:, 0:W2W],
                                    self.AA[:, fc * TW + tloc * 128:
                                            fc * TW + tloc * 128 + 128],
                                    sl[:, fc * W2W:(fc + 1) * W2W],
                                    start=(fc == 0), stop=(fc == NFC - 1))
                            return last

                        p.emit("pe", [f"ws{s}", "AA"], [psk], ymm)

                        xt = self.X[:, t * D + q * W2W: t * D + q * W2W + W2W]
                        cmb = self.SM[:, 384 + t * 8 + ei: 384 + t * 8 + ei + 1]

                        def ev(e, xt=xt, cmb=cmb, bank=bank):
                            return e.scalar_tensor_tensor(
                                xt, self.PS[bank][:, 0:W2W], cmb, xt,
                                ALU.mult, ALU.add)

                        p.emit("dve", [psk, "CMB", "X"], ["X"], ev)
                    ci += 1
                    ensure_loads(ci + 1)
        if layer == 1:
            self._ffn1_b2(wts)

    def _router0(self, wts):
        p = self.p
        gw = wts["gate_w"]
        p.emit_dma("pool", [], ["GWT"], lambda e: [
            e.dma_start(out=self.GWT[:].rearrange("p (c e2) -> p c e2", e2=8),
                        in_=gw.rearrange("(c p) e -> p c e", p=128))], stream="gw")

        for st in range(4):
            c0 = st * 512

            def lmm(e, c0=c0):
                last = None
                for dc in range(8):
                    last = e.matmul(self.PS[4][0:8, :],
                                    self.GWT[:, dc * 8:(dc + 1) * 8],
                                    self.AB[:, dc * NTOK + c0: dc * NTOK + c0 + 512],
                                    start=(dc == 0), stop=(dc == 7))
                return last

            p.emit("pe", ["GWT", "AB"], ["pb4"], lmm)
            p.emit("dve", ["pb4"], ["LGS"],
                   lambda e: e.tensor_copy(self.SMB[0:8, 0:512], self.PS[4][0:8, :]))

            def trf(e):
                last = None
                for q in range(4):
                    last = e.transpose(self.PS[5][:, q * 8:(q + 1) * 8],
                                       self.SMB[0:8, q * 128:(q + 1) * 128],
                                       self.IDTF[0:8, 0:8])
                return last

            p.emit("pe", ["LGS", "IDTF"], ["pb5"], trf)
            p.emit("dve", ["pb5"], ["LT"],
                   lambda e: e.tensor_copy(self.SM[:, 864:896], self.PS[5][:, 0:32]))

            for q in range(4):
                t = st * 4 + q
                self._top2_combine(self.SM[:, 864 + q * 8:864 + q * 8 + 8],
                                   self.SM[:, 384 + t * 8:384 + t * 8 + 8],
                                   softmax_top2=True)

    def _top2_combine(self, LT, cmb, softmax_top2):
        p = self.p
        SM, SMB = self.SM, self.SMB
        m1 = SM[:, 832:833]
        m2 = SM[:, 836:837]
        dlt = SM[:, 840:841]
        ex = SM[:, 844:845]
        war = SM[:, 848:849]
        wa = SM[:, 852:853]
        wb = SM[:, 856:857]
        ge1 = SMB[:, 512:520]
        ge2 = SMB[:, 528:536]
        l2 = SMB[:, 544:552]
        geA = SMB[:, 560:568]

        p.emit("dve", ["LT"], ["RTS"],
               lambda e: e.reduce_max(m1, LT, axis=AX), strict=True)

        p.emit("dve", ["RTS", "LT"], ["RTS"],
               lambda e: e.tensor_scalar(ge1, LT, m1, None, ALU.is_ge), strict=True)
        p.emit("dve", ["RTS", "LT"], ["RTS"],
               lambda e: e.scalar_tensor_tensor(l2, ge1, -1e30, LT, ALU.mult, ALU.add),
               strict=True)
        p.emit("dve", ["RTS"], ["RTS"],
               lambda e: e.reduce_max(m2, l2, axis=AX), strict=True)
        p.emit("dve", ["RTS"], ["RTS"],
               lambda e: e.tensor_tensor(dlt, m2, m1, ALU.subtract), strict=True)
        p.emit("act", ["RTS"], ["RTS"],
               lambda e: e.activation(ex, dlt, AF.Exp, bias=0.0, scale=1.0),
               strict=True)
        p.emit("dve", ["RTS"], ["RTS"],
               lambda e: e.tensor_scalar(war, ex, 1.0, None, ALU.add), strict=True)
        p.emit("dve", ["RTS"], ["RTS"],
               lambda e: e.reciprocal(war, war), strict=True)
        p.emit("dve", ["RTS"], ["RTS"],
               lambda e: e.tensor_scalar(wa, war, 0.5, None, ALU.mult), strict=True)
        p.emit("dve", ["RTS"], ["RTS"],
               lambda e: e.tensor_scalar(wb, wa, -1.0, 0.5, ALU.mult, ALU.add),
               strict=True)
        p.emit("dve", ["RTS", "LT"], ["RTS"],
               lambda e: e.tensor_scalar(geA, LT, m2, None, ALU.is_ge), strict=True)
        p.emit("dve", ["RTS"], ["RTS"],
               lambda e: e.tensor_tensor(ge2, geA, ge1, ALU.subtract), strict=True)
        p.emit("dve", ["RTS"], ["RTS", "CMBp"],
               lambda e: e.tensor_scalar(cmb, ge1, wa, None, ALU.mult), strict=True)
        p.emit("dve", ["RTS", "CMBp"], ["CMB"],
               lambda e: e.scalar_tensor_tensor(cmb, ge2, wb, cmb, ALU.mult, ALU.add),
               strict=True)

    def ln_tile_seq(self, x_ap, x_key, out_ap, out_key, scol=0):
        p = self.p
        SM = self.SM
        s1 = SM[:, scol:scol + 1]
        s2 = SM[:, scol + 4:scol + 5]
        nmean = SM[:, scol + 8:scol + 9]
        tmp = SM[:, scol + 12:scol + 13]
        var = SM[:, scol + 16:scol + 17]
        rstd = SM[:, scol + 20:scol + 21]
        sq_dump = self.SC[:, 0:D]
        p.emit("dve", [x_key], ["LNS"],
               lambda e: e.reduce_sum(s1, x_ap, axis=AX), strict=True)
        p.emit("act", [x_key, "LNS"], ["LNS", "SCs0"],
               lambda e: e.activation(sq_dump, x_ap, AF.Square, bias=0.0, scale=1.0,
                                      accum_out=s2), strict=True)
        p.emit("dve", ["LNS"], ["LNS"],
               lambda e: e.tensor_scalar(nmean, s1, -1.0 / D, None, ALU.mult),
               strict=True)
        p.emit("act", ["LNS"], ["LNS"],
               lambda e: e.activation(tmp, nmean, AF.Square, bias=0.0, scale=1.0),
               strict=True)
        p.emit("dve", ["LNS"], ["LNS"],
               lambda e: e.scalar_tensor_tensor(var, s2, 1.0 / D, tmp,
                                                ALU.mult, ALU.subtract), strict=True)
        p.emit("act", ["LNS", "CONSTS"], ["LNS"],
               lambda e: e.activation(rstd, var, AF.Ln, bias=1e-5, scale=1.0),
               strict=True)
        p.emit("act", ["LNS", "CONSTS"], ["LNS"],
               lambda e: e.activation(rstd, rstd, AF.Exp, bias=0.0, scale=-0.5),
               strict=True)
        p.emit("dve", [x_key, "LNS"], [out_key],
               lambda e: e.tensor_scalar(out_ap, x_ap, nmean, rstd, ALU.add, ALU.mult))
        return nmean, rstd

    def mamba(self, wts, partner_src, partner_bf16=False, partner_reads=()):
        p = self.p
        AA, SM, SMB, SC = self.AA, self.SM, self.SMB, self.SC
        NV = 32

        DFULLT = p.sb_at("DFULLT", [128, 1024], F32, self.W0)
        self.DFULLT = DFULLT
        DTF = p.sb_at("DTF", [8, 4096], F32, self.W0)
        DEAD = p.sb_at("DEAD", [8, 4096], F32, self.W0)
        HSEL = p.sb_at("HSEL", [8, 1024], F32, self.W0)
        WXT = p.sb_at("WXT", [128, 1088], BF16, self.W1)
        ACUM = p.sb_at("ACUM", [8, 4096], F32, self.W1)
        SSCR = p.sb_at("SSCR", [8, 4096], F32, self.AB)
        NADF = p.sb_at("NADF", [8, 4096], F32, self.AB)
        BF = p.sb_at("BF", [64, 4096], BF16, self.AB, 16384)
        CF = p.sb_at("CF", [64, 4096], BF16, self.AB, 24576)
        XSTG = p.sb_at("XSTG", [128, 1024], F32, self.SMB)
        BCT = p.sb_at("BCT", [128, 1024], F32, self.SMB)
        DTTM = SM[:, 384:640].rearrange("p (v h) -> p v h", h=8)
        NADTM = SM[:, 640:896].rearrange("p (v h) -> p v h", h=8)
        ALOG = SM[0:8, 896:897]
        NEGA = SM[0:8, 900:901]
        ST = p.sb("STm", [128, 512], BF16)
        LTL = p.sb("LTLm", [128, 512], BF16)
        TRI = p.sb("TRIm", [128, 128], BF16)

        wxt, dtb, alog = wts["WXT"], wts["dt_bias"], wts["A_log"]
        PS, PST = self.PS, self.PST

        p.emit_dma("pool", [], ["ws1"], lambda e: [
            e.dma_start(out=WXT[:].rearrange("p (c f) -> p c f", f=136),
                        in_=wxt.rearrange("(c p) f -> p c f", p=128))], stream="w1")
        p.emit_dma("sp", [], ["TRIm"], lambda e: [
            e.dma_start(out=TRI[:], in_=wts["TRI"])], stream="io")
        p.emit_dma("sp", [], ["Asm"], lambda e: [
            e.dma_start(out=ALOG, in_=alog)], stream="io")
        p.emit_dma("sp", [], ["ws0"], lambda e: [
            e.dma_start(out=DFULLT[:], in_=wts["DFULL"])], stream="io")

        for t in range(NT):
            xt = self.X[:, t * D:(t + 1) * D]
            aa = AA[:, (16 + t) * 1024:(17 + t) * 1024]
            nmean, rstd = self.ln_tile_seq(xt, "X", aa, "AA", scol=(t % 2) * 24)
            tmp = XSTG[:]

            def dterm1(e, xt=xt, nmean=nmean, tmp=tmp):
                return e.scalar_tensor_tensor(tmp, xt, nmean, self.DFULLT[:],
                                              ALU.add, ALU.mult)

            p.emit("dve", ["X", "LNS", "ws0"], ["SMB"], dterm1)

            def dterm2(e, xt=xt, rstd=rstd, tmp=tmp):
                return e.scalar_tensor_tensor(xt, tmp, rstd, xt, ALU.mult, ALU.add)

            p.emit("dve", ["SMB", "LNS", "X"], ["X"], dterm2)

        for v in range(NT):
            if partner_bf16:
                p.emit_dma("pool", list(partner_reads), ["SMB"], lambda e, v=v: [
                    e.dma_start(out=XSTG[:], in_=partner_src(v))], stream="xs")
            else:
                p.emit_dma("sp", list(partner_reads), ["SMB"], lambda e, v=v: [
                    e.dma_start(out=XSTG[:], in_=partner_src(v))], stream="xs")
            aa = AA[:, v * 1024:(v + 1) * 1024]
            self.ln_tile_seq(XSTG[:], "SMB", aa, "AA", scol=(v % 2) * 24)

        for v in range(NV):
            src = AA[:, v * 1024:(v + 1) * 1024]

            def tr(e, src=src):
                last = None
                for c in range(8):
                    last = e.transpose(PST[:, c * 128:(c + 1) * 128],
                                       src[:, c * 128:(c + 1) * 128], self.IDT[:])
                return last

            p.emit("pe", ["AA", "IDT"], ["PST"], tr)
            slot = v % 2
            xf = SC[:, slot * D:(slot + 1) * D]
            p.emit("dve", ["PST"], [f"SCs{slot}"],
                   lambda e, xf=xf: e.tensor_copy(
                       xf.rearrange("p (c f) -> p c f", f=128),
                       PST[:, 0:1024].rearrange("p (c f) -> p c f", f=128)))

            def mm(e, xf=xf):
                last = None
                for dc in range(8):
                    e.matmul(PS[4][0:8, 0:128], WXT[:, dc * 136:dc * 136 + 8],
                             xf[:, dc * 128:(dc + 1) * 128],
                             start=(dc == 0), stop=(dc == 7))
                for dc in range(8):
                    e.matmul(PS[4][0:64, 128:256], WXT[:, dc * 136 + 8:dc * 136 + 72],
                             xf[:, dc * 128:(dc + 1) * 128],
                             start=(dc == 0), stop=(dc == 7))
                for dc in range(8):
                    last = e.matmul(PS[4][0:64, 256:384],
                                    WXT[:, dc * 136 + 72:dc * 136 + 136],
                                    xf[:, dc * 128:(dc + 1) * 128],
                                    start=(dc == 0), stop=(dc == 7))
                return last

            p.emit("pe", [f"SCs{slot}", "ws1"], ["pb4"], mm)
            c0 = v * 128

            def ev(e, c0=c0):
                e.tensor_copy(DTF[:, c0:c0 + 128], PS[4][0:8, 0:128])
                e.tensor_copy(BF[:, c0:c0 + 128], PS[4][0:64, 128:256])
                return e.tensor_copy(CF[:, c0:c0 + 128], PS[4][0:64, 256:384])

            p.emit("dve", ["pb4"], ["ws0", "ABb", "ABc"], ev)

        DB = SM[0:8, 904:905]
        p.emit_dma("sp", [], ["Asm2"], lambda e: [
            e.dma_start(out=DB, in_=dtb)], stream="io")
        p.emit("dve", ["ws0", "Asm2"], ["ws0"],
               lambda e: e.tensor_scalar(DTF[:], DTF[:], DB, None, ALU.add))
        p.emit("act", ["ws0"], ["ABs"],
               lambda e: e.activation(SSCR[:], DTF[:], AF.Abs, bias=0.0, scale=1.0))
        p.emit("act", ["ABs", "CONSTS"], ["ABs"],
               lambda e: e.activation(SSCR[:], SSCR[:], AF.Exp, bias=0.0, scale=-1.0))
        p.emit("act", ["ABs"], ["ABs"],
               lambda e: e.activation(SSCR[:], SSCR[:], AF.Ln, bias=1.0, scale=1.0))
        p.emit("dve", ["ws0"], ["ws0"],
               lambda e: e.tensor_scalar(DTF[:], DTF[:], 0.0, None, ALU.max))
        p.emit("dve", ["ws0", "ABs"], ["ws0"],
               lambda e: e.tensor_tensor(DTF[:], DTF[:], SSCR[:], ALU.add))

        def trdt(e):
            last = None
            for v in range(NV):
                last = e.transpose(PS[4][:, v * 8:(v + 1) * 8],
                                   DTF[0:8, v * 128:(v + 1) * 128], self.IDTF[0:8, 0:8])
            return last

        p.emit("pe", ["ws0", "IDTF"], ["pb4"], trdt)
        p.emit("dve", ["pb4"], ["DTTM"],
               lambda e: e.tensor_copy(DTTM, PS[4][:, 0:256].rearrange(
                   "p (v h) -> p v h", h=8)))

        p.emit("act", ["Asm"], ["Asm"],
               lambda e: e.activation(NEGA, ALOG, AF.Exp, bias=0.0, scale=1.0),
               strict=True)
        p.emit("dve", ["Asm"], ["Asm"],
               lambda e: e.tensor_scalar(NEGA, NEGA, -1.0, None, ALU.mult),
               strict=True)
        p.emit("dve", ["ws0", "Asm", "DTTM"], ["ws0"],
               lambda e: e.tensor_scalar(DTF[:], DTF[:], NEGA, None, ALU.mult))
        p.emit("dve", ["ws0"], ["ws1"],
               lambda e: e.tensor_tensor_scan(ACUM[:], DTF[:], DTF[:], 0.0,
                                              ALU.add, ALU.bypass), strict=True)

        p.emit_dma("pool", ["ws0"], ["ws0"], lambda e: [
            e.dma_start(out=DEAD[:], in_=wts["DEADV"])], stream="w0")
        p.emit("dve", ["ws0", "ws1"], ["ABs"],
               lambda e: e.scalar_tensor_tensor(NADF[:], ACUM[:], -1.0, DEAD[:],
                                                ALU.mult, ALU.add))

        def trnad(e):
            last = None
            for v in range(NV):
                last = e.transpose(PS[4][:, v * 8:(v + 1) * 8],
                                   NADF[0:8, v * 128:(v + 1) * 128],
                                   self.IDTF[0:8, 0:8])
            return last

        p.emit("pe", ["ABs", "IDTF"], ["pb4"], trnad)
        p.emit("dve", ["pb4"], ["NADTM"],
               lambda e: e.tensor_copy(NADTM, PS[4][:, 0:256].rearrange(
                   "p (v h) -> p v h", h=8)))
        p.emit_dma("pool", ["ws0"], ["ws0"], lambda e: [
            e.dma_start(out=HSEL[:], in_=wts["HSEL"])], stream="w0")

        for v in range(NV):
            def xd(e, v=v):
                last = None
                for h in range(8):
                    last = e.tensor_scalar(
                        AA[:, v * 1024 + h * 128: v * 1024 + h * 128 + 128],
                        AA[:, v * 1024 + h * 128: v * 1024 + h * 128 + 128],
                        DTTM[:, v, h:h + 1].squeeze(1) if False else
                        SM[:, 384 + v * 8 + h:384 + v * 8 + h + 1],
                        None, ALU.mult)
                return last

            p.emit("dve", ["AA", "DTTM"], ["AA"], xd)

        for ls in range(4):
            jsv = 16 + ls * 4
            l0 = jsv * 128
            ns = jsv + 4
            for hg in range(4):
                heads = [hg * 2, hg * 2 + 1]
                for i, h in enumerate(heads):
                    def bcmm(e, h=h, l0=l0):
                        return e.matmul(PS[6][:], HSEL[:, h * 128:(h + 1) * 128],
                                        ACUM[:, l0:l0 + 512], start=True, stop=True)

                    p.emit("pe", ["ws0", "ws1"], ["pb6"], bcmm)
                    p.emit("dve", ["pb6"], ["SMB"],
                           lambda e, i=i: e.tensor_copy(BCT[:, i * 512:(i + 1) * 512],
                                                        PS[6][:]))
                ydone = []
                for si in range(ns):
                    def gmm(e, si=si, l0=l0):
                        return e.matmul(PS[5][:], BF[:, si * 128:(si + 1) * 128],
                                        CF[:, l0:l0 + 512], start=True, stop=True)

                    p.emit("pe", ["ABb", "ABc"], ["pb5"], gmm)
                    diag = (jsv <= si < jsv + 4)
                    for i, h in enumerate(heads):
                        nad = SM[:, 640 + si * 8 + h:640 + si * 8 + h + 1]
                        bch = BCT[:, i * 512:(i + 1) * 512]
                        if not diag:
                            p.emit("act", ["SMB", "NADTM"], ["LTLm"],
                                   lambda e, bch=bch, nad=nad: e.activation(
                                       LTL[:], bch, AF.Exp, bias=nad, scale=1.0))
                        else:
                            p.emit("dve", ["SMB", "NADTM"], ["LTLm"],
                                   lambda e, bch=bch, nad=nad: e.tensor_scalar(
                                       LTL[:], bch, nad, 0.0, ALU.add, ALU.min))
                            p.emit("act", ["LTLm"], ["LTLm"],
                                   lambda e: e.activation(LTL[:], LTL[:], AF.Exp,
                                                          bias=0.0, scale=1.0))
                        p.emit("dve", ["pb5", "LTLm"], ["STm"],
                               lambda e: e.tensor_tensor(ST[:], PS[5][:], LTL[:],
                                                         ALU.mult))
                        if diag:
                            q = si - jsv
                            if q > 0:
                                p.emit("dve", ["STm"], ["STm"],
                                       lambda e, q=q: e.memset(ST[:, 0:q * 128], 0.0))
                            p.emit("dve", ["STm", "TRIm"], ["STm"],
                                   lambda e, q=q: e.tensor_tensor(
                                       ST[:, q * 128:(q + 1) * 128],
                                       ST[:, q * 128:(q + 1) * 128],
                                       TRI[:], ALU.mult))
                        def ymm(e, si=si, i=i, h=h, first=(si == 0), last=(si == ns - 1)):
                            lastmm = None
                            for lq in range(4):
                                lastmm = e.matmul(
                                    PS[i][:, lq * 128:(lq + 1) * 128],
                                    ST[:, lq * 128:(lq + 1) * 128],
                                    AA[:, si * 1024 + h * 128: si * 1024 + h * 128 + 128],
                                    start=(first and lq == 0),
                                    stop=(last and lq == 3))
                            return lastmm

                        p.emit("pe", ["STm", "AA"], [f"pb{i}"], ymm)
                for i, h in enumerate(heads):
                    for lq in range(4):
                        t = ls * 4 + lq
                        xt = self.X[:, t * D + h * 128: t * D + h * 128 + 128]

                        def ev(e, xt=xt, i=i, lq=lq):
                            return e.tensor_tensor(xt, PS[i][:, lq * 128:(lq + 1) * 128],
                                                   xt, ALU.add)

                        p.emit("dve", [f"pb{i}", "X"], ["X"], ev)

    def _router1(self, wts):
        p = self.p
        SM, SMB = self.SM, self.SMB
        PS = self.PS

        B1S = p.sb_at("B1S", [8, 4096], F32, self.W1)
        p.emit_dma("sp", [], ["ws1"], lambda e: [
            e.dma_start(out=B1S[:], in_=wts["b1"])], stream="io")

        def trb1(e):
            last = None
            for fc in range(32):
                last = e.transpose(PS[4][:, fc * 8:(fc + 1) * 8],
                                   B1S[:, fc * 128:(fc + 1) * 128],
                                   self.IDTF[0:8, 0:8])
            return last

        p.emit("pe", ["ws1", "IDTF"], ["pb4"], trb1)
        p.emit("dve", ["pb4"], ["B1T"],
               lambda e: e.tensor_copy(SM[:, 512:768], PS[4][:, 0:256]))

        p.emit_dma("pool", [], ["GWT"], lambda e: [
            e.dma_start(out=self.GWT[:].rearrange("p (c e2) -> p c e2", e2=8),
                        in_=wts["router_w"].rearrange("(c p) e -> p c e", p=128))],
            stream="gw")
        RB = SM[0:8, 908:909]
        p.emit_dma("sp", [], ["Asm3"], lambda e: [
            e.dma_start(out=RB, in_=wts["router_b"])], stream="io")

        MSKF = p.sb_at("MSKF", [8, 2048], F32, self.W0)
        PSF = p.sb_at("PSF", [8, 2048], F32, self.W0, 8192)
        NEGM1 = SM[:, 912:913]
        RCP = SM[:, 916:917]
        m1 = SM[:, 832:833]
        m2 = SM[:, 836:837]
        SACC = SM[:, 920:921]
        GTS = SMB[:, 576:584]
        ge1 = SMB[:, 640:648]
        l2 = SMB[:, 656:664]

        for st in range(4):
            c0 = st * 512

            def lmm(e, c0=c0):
                last = None
                for dc in range(8):
                    last = e.matmul(PS[4][0:8, :],
                                    self.GWT[:, dc * 8:(dc + 1) * 8],
                                    self.AB[:, dc * NTOK + c0: dc * NTOK + c0 + 512],
                                    start=(dc == 0), stop=(dc == 7))
                return last

            p.emit("pe", ["GWT", "AB"], ["pb4"], lmm)
            p.emit("dve", ["pb4", "Asm3"], ["LGS"],
                   lambda e: e.tensor_scalar(SMB[0:8, 0:512], PS[4][0:8, :],
                                             RB, None, ALU.add))

            def trf(e):
                last = None
                for q in range(4):
                    last = e.transpose(PS[5][:, q * 8:(q + 1) * 8],
                                       SMB[0:8, q * 128:(q + 1) * 128],
                                       self.IDTF[0:8, 0:8])
                return last

            p.emit("pe", ["LGS", "IDTF"], ["pb5"], trf)
            p.emit("dve", ["pb5"], ["LT"],
                   lambda e: e.tensor_copy(self.SM[:, 864:896], self.PS[5][:, 0:32]))

            for q in range(4):
                t = st * 4 + q
                LT = SM[:, 864 + q * 8:864 + q * 8 + 8]
                gates = SM[:, 384 + t * 8:384 + t * 8 + 8]
                msk = SMB[:, 768 + t * 8:768 + t * 8 + 8]
                p.emit("dve", ["LT"], ["RTS"],
                       lambda e, LT=LT: e.reduce_max(m1, LT, axis=AX), strict=True)
                p.emit("dve", ["RTS"], ["RTS"],
                       lambda e: e.tensor_scalar(NEGM1, m1, -1.0, None, ALU.mult),
                       strict=True)
                p.emit("act", ["RTS", "LT"], ["RTS"],
                       lambda e, LT=LT: e.activation(GTS, LT, AF.Exp, bias=NEGM1,
                                                     scale=1.0, accum_out=SACC),
                       strict=True)
                p.emit("dve", ["RTS"], ["RTS"],
                       lambda e: e.reciprocal(RCP, SACC), strict=True)
                p.emit("dve", ["RTS"], ["CMB"],
                       lambda e, gates=gates: e.tensor_scalar(gates, GTS, RCP,
                                                              None, ALU.mult))
                p.emit("dve", ["RTS", "LT"], ["RTS"],
                       lambda e, LT=LT: e.tensor_scalar(ge1, LT, m1, None, ALU.is_ge),
                       strict=True)
                p.emit("dve", ["RTS", "LT"], ["RTS"],
                       lambda e, LT=LT: e.scalar_tensor_tensor(
                           l2, ge1, -1e30, LT, ALU.mult, ALU.add), strict=True)
                p.emit("dve", ["RTS"], ["RTS"],
                       lambda e: e.reduce_max(m2, l2, axis=AX), strict=True)
                p.emit("dve", ["RTS", "LT"], ["MSKT"],
                       lambda e, LT=LT, msk=msk: e.tensor_scalar(
                           msk, LT, m2, None, ALU.is_ge), strict=True)

        for g in range(4):
            def trmsk(e, g=g):
                last = None
                for q in range(4):
                    t = g * 4 + q
                    last = e.transpose(PS[4][0:8, q * 128:(q + 1) * 128],
                                       SMB[:, 768 + t * 8:768 + t * 8 + 8],
                                       self.IDTF[:])
                return last

            p.emit("pe", ["MSKT", "IDTF"], ["pb4"], trmsk)
            p.emit("dve", ["pb4"], ["ws0"],
                   lambda e, g=g: e.tensor_copy(MSKF[:, g * 512:(g + 1) * 512],
                                                PS[4][0:8, 0:512]))
        p.emit("dve", ["ws0"], ["ws0"],
               lambda e: e.tensor_tensor_scan(PSF[:], MSKF[:], MSKF[:], 0.0,
                                              ALU.add, ALU.bypass), strict=True)
        TOT = SM[0:8, 924:925]
        p.emit("dve", ["ws0"], ["Atot"],
               lambda e: e.tensor_copy(TOT, PSF[:, 2047:2048]), strict=True)
        cc_in = p.dtmp("cc_cnt_in" + wts["tag"], [1, 8], F32)
        cc_out = p.dtmp("cc_cnt_out" + wts["tag"], [8, 8], F32)
        p.emit_dma("pool", ["Atot"], ["ccin"], lambda e: [
            e.dma_start(out=cc_in.rearrange("one e -> e one"), in_=TOT)], stream="cc")
        p.emit_cc(["ccin"], ["ccout"], lambda e: e.collective_compute(
            "AllGather", ALU.bypass, replica_groups=[list(range(8))],
            ins=[cc_in.tensor.ap().opt()], outs=[cc_out.tensor.ap().opt()]))
        AGO = SMB[0:8, 688:696]
        p.emit_dma("pool", ["ccout"], ["AGO"], lambda e: [
            e.dma_start(out=AGO, in_=cc_out)], stream="cc")
        OFFS = SM[0:8, 928:929]
        self.PREF = SM[0:8, 932:933]
        p.emit_dma("sp", [], ["PREF"], lambda e: [
            e.dma_start(out=self.PREF, in_=wts["prefmask"])], stream="io")

        def offmm(e):
            return e.matmul(PS[6][0:8, 0:1], AGO, self.PREF, start=True, stop=True)

        p.emit("pe", ["AGO", "PREF"], ["pb6"], offmm)
        p.emit("dve", ["pb6"], ["Aoff"],
               lambda e: e.tensor_copy(OFFS, PS[6][0:8, 0:1]), strict=True)
        p.emit("dve", ["ws0", "Aoff"], ["ws0"],
               lambda e: e.tensor_scalar(PSF[:], PSF[:], OFFS, None, ALU.add))
        p.emit("dve", ["ws0"], ["ws0"],
               lambda e: e.tensor_scalar(PSF[:], PSF[:], 2560.0, None, ALU.is_le))
        def trvc(e):
            last = None
            for t in range(NT):
                last = e.transpose(PS[4][:, t * 8:(t + 1) * 8],
                                   PSF[0:8, t * 128:(t + 1) * 128],
                                   self.IDTF[0:8, 0:8])
            return last

        p.emit("pe", ["ws0", "IDTF"], ["pb4"], trvc)
        p.emit("dve", ["pb4"], ["VCT"],
               lambda e: e.tensor_copy(SMB[:, 896:1024], PS[4][:, 0:128]))
        p.emit("dve", ["CMB", "MSKT"], ["CMB"],
               lambda e: e.tensor_tensor(SM[:, 384:512], SM[:, 384:512],
                                         SMB[:, 768:896], ALU.mult))
        p.emit("dve", ["CMB", "VCT"], ["CMB"],
               lambda e: e.tensor_tensor(SM[:, 384:512], SM[:, 384:512],
                                         SMB[:, 896:1024], ALU.mult))
        p.emit("dve", ["CMB"], ["CMB"],
               lambda e: e.tensor_scalar(SM[:, 384:512], SM[:, 384:512], 0.5,
                                         None, ALU.mult))

    def _ffn1_b2(self, wts):
        p = self.p
        SM = self.SM
        PS = self.PS
        WFM = p.sb_at("WFM", [8, 2048], F32, self.W0)
        B2S = p.sb_at("B2S", [8, 1024], F32, self.W1)
        p.emit_dma("sp", [], ["ws1"], lambda e: [
            e.dma_start(out=B2S[:], in_=wts["b2"])], stream="io")

        for g in range(4):
            def trw(e, g=g):
                last = None
                for q in range(4):
                    t = g * 4 + q
                    last = e.transpose(PS[4][0:8, q * 128:(q + 1) * 128],
                                       SM[:, 384 + t * 8:384 + t * 8 + 8],
                                       self.IDTF[:])
                return last

            p.emit("pe", ["CMB", "IDTF"], ["pb4"], trw)
            p.emit("dve", ["pb4"], ["ws0"],
                   lambda e, g=g: e.tensor_copy(WFM[:, g * 512:(g + 1) * 512],
                                                PS[4][0:8, 0:512]))
        for t in range(NT):
            for dh in range(2):
                def bmm(e, t=t, dh=dh):
                    return e.matmul(PS[5][:], WFM[:, t * 128:(t + 1) * 128],
                                    B2S[:, dh * 512:(dh + 1) * 512],
                                    start=True, stop=True)

                p.emit("pe", ["ws0", "ws1"], ["pb5"], bmm)
                xt = self.X[:, t * D + dh * 512: t * D + dh * 512 + 512]

                def ev(e, xt=xt):
                    return e.tensor_tensor(xt, PS[5][:], xt, ALU.add)

                p.emit("dve", ["pb5", "X"], ["X"], ev)


    def attn1(self, wts, view_src, view_normed, xn_sink=None, post_sink=None, view_reads=()):
        p = self.p
        AA, AB, SM, SMB, SC = self.AA, self.AB, self.SM, self.SMB, self.SC
        PS = self.PS

        MQ = p.sb_at("MQ", [128, 4096], BF16, self.W0)
        MK = p.sb_at("MK", [128, 4096], BF16, self.W0, 8192)
        MV = p.sb_at("MV", [128, 8192], BF16, self.W1)
        XSTG = p.sb_at("XSTGa", [128, 1024], F32, self.SMB)
        p.emit_dma("pool", [], ["ws0"], lambda e: [
            e.dma_start(out=MQ[:].rearrange("p (c f) -> p c f", f=512),
                        in_=wts["MQ"].rearrange("(c p) f -> p c f", p=128)),
            e.dma_start(out=MK[:].rearrange("p (c f) -> p c f", f=512),
                        in_=wts["MK"].rearrange("(c p) f -> p c f", p=128)),
        ], n=2, stream="w0")
        p.emit_dma("pool", [], ["ws1"], lambda e: [
            e.dma_start(out=MV[:].rearrange("p (c f) -> p c f", f=1024),
                        in_=wts["MV"].rearrange("(c p) f -> p c f", p=128)),
        ], stream="w1")

        self.ln_stats_all()
        for t in range(NT):
            slot = t % 2
            sck = self.ln_apply(t, slot)
            srcsc = SC[:, slot * D:(slot + 1) * D]

            def tr(e, srcsc=srcsc):
                last = None
                for c in range(8):
                    last = e.transpose(self.PST[:, c * 128:(c + 1) * 128],
                                       srcsc[:, c * 128:(c + 1) * 128], self.IDT[:])
                return last

            if xn_sink is not None:
                p.emit_dma("sp", [sck], ["E2IN"], lambda e, t=t, sck=sck, srcsc=srcsc: [
                    e.dma_start(out=xn_sink(t), in_=srcsc)], stream="xo")
            p.emit("pe", [sck, "IDT"], ["PST"], tr)
            xf = SC[:, 0:D] if slot == 1 else SC[:, D:2 * D]
            xfk = "SCs0" if slot == 1 else "SCs1"
            p.emit("dve", ["PST"], [xfk],
                   lambda e, xf=xf: e.tensor_copy(
                       xf.rearrange("p (c f) -> p c f", f=128),
                       self.PST[:, 0:D].rearrange("p (c f) -> p c f", f=128)))

            for dst, W, pbk, bank in (("ABq", MQ, "pb2", 2), ("ABk", MK, "pb3", 3)):
                def qmm(e, xf=xf, W=W, bank=bank):
                    last = None
                    for qc in range(4):
                        for dc in range(8):
                            last = e.matmul(
                                PS[bank][:, qc * 128:(qc + 1) * 128],
                                W[:, dc * 512 + qc * 128:dc * 512 + qc * 128 + 128],
                                xf[:, dc * 128:(dc + 1) * 128],
                                start=(dc == 0), stop=(dc == 7))
                    return last

                p.emit("pe", [xfk, "ws0"], [pbk], qmm)
                off = 0 if dst == "ABq" else 8192
                p.emit("act", [pbk], [dst],
                       lambda e, t=t, off=off, bank=bank: e.copy(
                           AB[:, off:off + 8192].rearrange(
                               "p (c n) -> p c n", n=2048)[:, :, t * 128:(t + 1) * 128],
                           PS[bank][:, 0:512].rearrange("p (c n) -> p c n", n=128)))

        if post_sink is not None:
            post_sink()
        for v in range(32):
            if not view_normed:
                p.emit_dma("sp", list(view_reads), ["SMB"], lambda e, v=v: [
                    e.dma_start(out=XSTG[:], in_=view_src(v))], stream="xs")
                self.ln_tile_seq(XSTG[:], "SMB", SC[:, D:2 * D], "SCs1",
                                 scol=(v % 2) * 24)
            else:
                p.emit_dma("sp", list(view_reads), ["SCs1"], lambda e, v=v: [
                    e.dma_start(out=SC[:, D:2 * D], in_=view_src(v))], stream="xs")

            def trv(e):
                last = None
                for c in range(8):
                    last = e.transpose(self.PST[:, c * 128:(c + 1) * 128],
                                       SC[:, D + c * 128:D + (c + 1) * 128],
                                       self.IDT[:])
                return last

            p.emit("pe", ["SCs1", "IDT"], ["PST"], trv)
            p.emit("dve", ["PST"], ["SCs0"],
                   lambda e: e.tensor_copy(
                       SC[:, 0:D].rearrange("p (c f) -> p c f", f=128),
                       self.PST[:, 0:D].rearrange("p (c f) -> p c f", f=128)))
            xf = SC[:, 0:D]
            for vh in range(2):
                def vmm1(e, xf=xf, vh=vh):
                    last = None
                    for dc in range(8):
                        last = e.matmul(PS[6][:],
                                        xf[:, dc * 128:(dc + 1) * 128],
                                        MV[:, dc * 1024 + vh * 512:dc * 1024 + vh * 512 + 512],
                                        start=(dc == 0), stop=(dc == 7))
                    return last

                p.emit("pe", ["SCs0", "ws1"], ["pb6"], vmm1)
                p.emit("act", ["pb6"], ["AA"],
                       lambda e, v=v, vh=vh: e.copy(
                           AA[:, v * 1024 + vh * 512: v * 1024 + vh * 512 + 512],
                           PS[6][:]))

        QH = p.sb_at("QH", [128, 4096], BF16, self.W0)
        KH = p.sb_at("KH", [128, 4096], BF16, self.W0, 8192)

        def kslab(cb, tok0, n):
            return KH[0:64, cb * 512 + tok0: cb * 512 + tok0 + n]

        def qslab(qb, tok0, n):
            return QH[0:64, qb * 512 + tok0: qb * 512 + tok0 + n]

        DP8s = (SM[:, 0:32:4], SM[:, 32:64:4])
        DDs = (SM[:, 64:65], SM[:, 68:69])
        LDs = (SM[:, 72:73], SM[:, 76:77])
        p.emit("dve", ["LNS"], ["LNS0", "LNS1"],
               lambda e: e.memset(SM[:, 0:80], 0.0))
        for hl in range(4):
            t0 = hl * 512
            for (srck, dsth, off) in (("ABq", QH, 0), ("ABk", KH, 8192)):
                for sb_i in range(8):
                    ch, base = sb_i // 2, (sb_i % 2) * 64

                    def exmm(e, ch=ch, base=base, off=off, t0=t0):
                        return e.matmul(PS[6][0:64, :],
                                        self.IDT[:, base:base + 64],
                                        AB[:, off + ch * 2048 + t0:
                                           off + ch * 2048 + t0 + 512],
                                        start=True, stop=True)

                    p.emit("pe", [srck, "IDT"], ["pb6"], exmm)
                    p.emit("dve", ["pb6"], ["ws0"],
                           lambda e, dsth=dsth, sb_i=sb_i: e.tensor_copy(
                               dsth[0:64, sb_i * 512:(sb_i + 1) * 512],
                               PS[6][0:64, :]))
            for ci in range(32):
                cb, ctr = ci // 4, (ci % 4) * 128
                par = ci % 2
                DP8, DD, LD = DP8s[par], DDs[par], LDs[par]
                lnsk = f"LNS{par}"
                for qb in range(8):
                    bank = 4 + (qb % 2)

                    def smm2(e, cb=cb, ctr=ctr, qb=qb, bank=bank):
                        return e.matmul(PS[bank][:], kslab(cb, ctr, 128),
                                        qslab(qb, 0, 512), start=True, stop=True)

                    p.emit("pe", ["ws0"], [f"pb{bank}"], smm2)
                    p.emit("act", [f"pb{bank}", lnsk, "CONSTS"], [lnsk, "SCs0"],
                           lambda e, qb=qb, bank=bank, par=par: e.activation(
                               SC[:, 0:512], PS[bank][:], AF.Exp, bias=0.0,
                               scale=0.125,
                               accum_out=SM[:, par * 32 + qb * 4:
                                            par * 32 + qb * 4 + 1]),
                           strict=True)
                p.emit("dve", [lnsk], [lnsk],
                       lambda e, DD=DD, DP8=DP8: e.reduce_sum(DD, DP8, axis=AX),
                       strict=True)
                p.emit("act", [lnsk], [lnsk],
                       lambda e, LD=LD, DD=DD: e.activation(LD, DD, AF.Ln,
                                                            bias=0.0, scale=1.0),
                       strict=True)
                p.emit("dve", [lnsk], ["CMB"],
                       lambda e, ci=ci, LD=LD: e.tensor_scalar(
                           SM[:, 384 + ci * 4:384 + ci * 4 + 1], LD, -1.0, None,
                           ALU.mult), strict=True)
            for lt in range(4):
                tloc = hl * 4 + lt

                def emit_p(ci):
                    cb, ctr = ci // 4, (ci % 4) * 128
                    par = ci % 2
                    for qb in range(8):
                        bank = 4 + (qb % 2)

                        def smm3(e, cb=cb, ctr=ctr, qb=qb, lt=lt, bank=bank):
                            return e.matmul(PS[bank][:, 0:128],
                                            kslab(cb, ctr, 128),
                                            qslab(qb, lt * 128, 128),
                                            start=True, stop=True)

                        p.emit("pe", ["ws0"], [f"pb{bank}"], smm3)
                        p.emit("act", [f"pb{bank}", "CMB", "CONSTS"],
                               [f"SCs{par}"],
                               lambda e, ci=ci, qb=qb, bank=bank, par=par:
                               e.activation(
                                   SC[:, par * 1024 + qb * 128:
                                      par * 1024 + (qb + 1) * 128],
                                   PS[bank][:, 0:128], AF.Exp,
                                   bias=SM[:, 384 + ci * 4:384 + ci * 4 + 1],
                                   scale=0.125))

                def emit_av(ci):
                    cb = ci // 4
                    par = ci % 2

                    def avmm(e, ci=ci, cb=cb, lt=lt, par=par):
                        last = None
                        for h2 in range(8):
                            pbank = h2 // 4
                            last = e.matmul(
                                PS[pbank][:, (h2 % 4) * 128:(h2 % 4) * 128 + 128],
                                SC[:, par * 1024 + h2 * 128:
                                   par * 1024 + (h2 + 1) * 128],
                                AA[:, (h2 * 4 + ci % 4) * 1024 + cb * 128:
                                   (h2 * 4 + ci % 4) * 1024 + cb * 128 + 128],
                                start=(ci == 0 and h2 % 4 == 0),
                                stop=(ci == 31 and h2 % 4 == 3))
                        return last

                    p.emit("pe", [f"SCs{par}", "AA"], ["pb0", "pb1"], avmm)

                for ci in range(32):
                    emit_p(ci)
                    if ci >= 1:
                        emit_av(ci - 1)
                emit_av(31)
                for pbank in range(2):
                    xt = self.X[:, tloc * D + pbank * 512: tloc * D + pbank * 512 + 512]

                    def ev(e, xt=xt, pbank=pbank):
                        return e.tensor_tensor(xt, PS[pbank][:], xt, ALU.add)

                    p.emit("dve", [f"pb{pbank}", "X"], ["X"], ev)

def _host_consts():
    cv = np.zeros((128, 16), dtype=np.float32)
    for i, v in enumerate(CONST_VALS):
        cv[:, i] = v
    return {
        "ident": np.eye(128, dtype=ml_dtypes.bfloat16),
        "identf": np.eye(128, dtype=np.float32),
        "ones": np.ones((1, 128), dtype=ml_dtypes.bfloat16),
        "constv": cv,
    }


def build_ffn0_launch(debug=False):
    p = Prog()
    mk = MK(p)
    x_in = p.din("x_in", [NTOK, D], F32)
    ident = p.din("ident", [128, 128], BF16)
    identf = p.din("identf", [128, 128], F32)
    ones = p.din("ones", [1, 128], BF16)
    constv = p.din("constv", [128, 16], F32)
    gate_w = p.din("gate_w", [D, 8], F32)
    w1 = p.din("w1", [8, D, 2048], F32)
    w2 = p.din("w2", [8, 2048, D], F32)
    x_out = p.dout("x_out", [NTOK, D], F32)
    if debug:
        dbg_ab = p.dout("dbg_ab", [128, 16384], BF16)
        dbg_cmb = p.dout("dbg_cmb", [128, 128], F32)
        dbg_h = p.dout("dbg_h", [128, 32768], BF16)
    mk.load_const(ident, identf, ones, constv)
    mk.load_x(x_in)
    mk.ffn(0, {"gate_w": gate_w, "w1": w1, "w2": w2},
           debug=({"ab": dbg_ab, "cmb": dbg_cmb, "h": dbg_h} if debug else None))
    mk.store_x(x_out)
    mk.finish()
    return p


def _mamba_consts(core):
    half = core % 2
    dead = np.zeros((8, 4096), np.float32)
    if half == 0:
        dead[:, :2048] = -1e30
    hsel = np.zeros((8, 1024), np.float32)
    for h in range(8):
        hsel[h, h * 128:(h + 1) * 128] = 1.0
    tri = np.tril(np.ones((128, 128), np.float32)).T
    tri = np.ascontiguousarray(np.tril(np.ones((128, 128), np.float32), 0).T)
    return dead, hsel, tri.astype(ml_dtypes.bfloat16)


def build_mamba_launch():
    p = Prog()
    mk = MK(p)
    x_in = p.din("x_in", [NTOK, D], F32)
    x_pr = p.din("x_pr", [NTOK, D], F32)
    ident = p.din("ident", [128, 128], BF16)
    identf = p.din("identf", [128, 128], F32)
    ones = p.din("ones", [1, 128], BF16)
    constv = p.din("constv", [128, 16], F32)
    wxt = p.din("wxt", [D, 136], F32)
    dtb = p.din("dt_bias", [8, 1], F32)
    alog = p.din("A_log", [8, 1], F32)
    dfull = p.din("dfull", [128, D], F32)
    deadv = p.din("deadv", [8, 4096], F32)
    hsel = p.din("hsel", [8, 1024], F32)
    tri = p.din("tri", [128, 128], BF16)
    x_out = p.dout("x_out", [NTOK, D], F32)
    mk.load_const(ident, identf, ones, constv)
    mk.load_x(x_in)
    xpr3 = x_pr.rearrange("(v p) d -> v p d", p=128)
    mk.mamba({"WXT": wxt, "dt_bias": dtb, "A_log": alog, "DFULL": dfull,
              "DEADV": deadv, "HSEL": hsel, "TRI": tri},
             lambda v: xpr3[v])
    mk.store_x(x_out)
    mk.finish()
    return p


def build_ffn1_launch(tag="a"):
    p = Prog()
    mk = MK(p)
    x_in = p.din("x_in", [NTOK, D], F32)
    ident = p.din("ident", [128, 128], BF16)
    identf = p.din("identf", [128, 128], F32)
    ones = p.din("ones", [1, 128], BF16)
    constv = p.din("constv", [128, 16], F32)
    router_w = p.din("router_w", [D, 8], F32)
    router_b = p.din("router_b", [8, 1], F32)
    b1 = p.din("b1", [8, 4096], F32)
    b2 = p.din("b2", [8, 1024], F32)
    w1 = p.din("w1", [8, D, 4096], F32)
    w2 = p.din("w2", [8, 4096, D], F32)
    prefmask = p.din("prefmask", [8, 1], F32)
    x_out = p.dout("x_out", [NTOK, D], F32)
    mk.load_const(ident, identf, ones, constv)
    mk.load_x(x_in)
    mk.ffn(1, {"router_w": router_w, "router_b": router_b, "b1": b1, "b2": b2,
               "w1": w1, "w2": w2, "prefmask": prefmask, "tag": tag})
    mk.store_x(x_out)
    mk.finish()
    return p


def build_attn1_launch():
    p = Prog()
    mk = MK(p)
    x_in = p.din("x_in", [NTOK, D], F32)
    x_seq = p.din("x_seq", [SEQ, D], F32)
    ident = p.din("ident", [128, 128], BF16)
    identf = p.din("identf", [128, 128], F32)
    ones = p.din("ones", [1, 128], BF16)
    constv = p.din("constv", [128, 16], F32)
    mq = p.din("MQ", [D, 512], F32)
    mk_in = p.din("MK", [D, 512], F32)
    mv = p.din("MV", [D, 1024], F32)
    x_out = p.dout("x_out", [NTOK, D], F32)
    mk.load_const(ident, identf, ones, constv)
    mk.load_x(x_in)
    xs3 = x_seq.rearrange("(v p) d -> v p d", p=128)
    mk.attn1({"MQ": mq, "MK": mk_in, "MV": mv}, lambda v: xs3[v], view_normed=False)
    mk.store_x(x_out)
    mk.finish()
    return p


def build_fused(upto=6):
    p = Prog()
    mk = MK(p)
    x_in = p.din("x_in", [NTOK, D], F32)
    ident = p.din("ident", [128, 128], BF16)
    identf = p.din("identf", [128, 128], F32)
    ones = p.din("ones", [1, 128], BF16)
    constv = p.din("constv", [128, 16], F32)
    gate_w = p.din("gate_w", [D, 8], F32)
    w1_0 = p.din("w1_0", [8, D, 2048], F32)
    w2_0 = p.din("w2_0", [8, 2048, D], F32)
    wxt = p.din("wxt", [D, 136], F32)
    dtb = p.din("dt_bias", [8, 1], F32)
    alog = p.din("A_log", [8, 1], F32)
    dfull = p.din("dfull", [128, D], F32)
    deadv = p.din("deadv", [8, 4096], F32)
    hsel = p.din("hsel", [8, 1024], F32)
    tri = p.din("tri", [128, 128], BF16)
    router_w = p.din("router_w", [D, 8], F32)
    router_b = p.din("router_b", [8, 1], F32)
    b1 = p.din("b1", [8, 4096], F32)
    b2 = p.din("b2", [8, 1024], F32)
    w1_1 = p.din("w1_1", [8, D, 4096], F32)
    w2_1 = p.din("w2_1", [8, 4096, D], F32)
    prefmask = p.din("prefmask", [8, 1], F32)
    mq = p.din("MQ", [D, 512], F32)
    mk_in = p.din("MK", [D, 512], F32)
    mv = p.din("MV", [D, 1024], F32)
    x_out = p.dout("x_out", [NTOK, D], F32)

    PAIRS = [[0, 1], [2, 3], [4, 5], [6, 7]]
    e1_in = p.dtmp("e1_in", [NTOK, D], F32)
    e1_out = p.dtmp("e1_out", [SEQ, D], F32)
    e2_in = p.dtmp("e2_in", [NTOK, D], BF16)
    e2_out = p.dtmp("e2_out", [SEQ, D], BF16)

    mk.load_const(ident, identf, ones, constv)
    mk.load_x(x_in)

    if upto >= 1:
        mk.ffn(0, {"gate_w": gate_w, "w1": w1_0, "w2": w2_0})

    p.emit_dma("sp", ["X"], ["E1IN"], lambda e: [
        e.dma_start(out=e1_in.rearrange("(t p) d -> p t d", p=128),
                    in_=mk.X[:].rearrange("p (t d) -> p t d", d=D))], stream="xo")
    p.emit_cc(["E1IN"], ["E1OUT"], lambda e: e.collective_compute(
        "AllGather", ALU.bypass, replica_groups=PAIRS,
        ins=[e1_in.tensor.ap().opt()], outs=[e1_out.tensor.ap().opt()]))
    e1v = e1_out.rearrange("(v p) d -> v p d", p=128)

    def mamba_partner(v):
        return e1v[v]

    if upto >= 2:
        mk.mamba({"WXT": wxt, "dt_bias": dtb, "A_log": alog, "DFULL": dfull,
                  "DEADV": deadv, "HSEL": hsel, "TRI": tri},
                 mamba_partner, partner_bf16=False, partner_reads=("E1OUT",))

    if upto >= 3:
        mk.ffn(0, {"gate_w": gate_w, "w1": w1_0, "w2": w2_0})

    if upto >= 4:
        mk.ffn(1, {"router_w": router_w, "router_b": router_b, "b1": b1, "b2": b2,
                   "w1": w1_1, "w2": w2_1, "prefmask": prefmask, "tag": "a"})

    e2i = e2_in.rearrange("(t p) d -> t p d", p=128)
    e2v = e2_out.rearrange("(v p) d -> v p d", p=128)

    def post_sink():
        p.emit_cc(["E2IN"], ["E2OUT"], lambda e: e.collective_compute(
            "AllGather", ALU.bypass, replica_groups=PAIRS,
            ins=[e2_in.tensor.ap().opt()], outs=[e2_out.tensor.ap().opt()]))

    if upto >= 5:
        mk.attn1({"MQ": mq, "MK": mk_in, "MV": mv},
                 lambda v: e2v[v], view_normed=True,
                 xn_sink=lambda t: e2i[t], post_sink=post_sink,
                 view_reads=("E2OUT",))

    if upto >= 6:
        mk.ffn(1, {"router_w": router_w, "router_b": router_b, "b1": b1, "b2": b2,
                   "w1": w1_1, "w2": w2_1, "prefmask": prefmask, "tag": "b"})

    mk.store_x(x_out)
    mk.finish()
    return p


def kernel(**inputs):
    consts = _host_consts()
    f32 = np.float32
    inp = {k: np.ascontiguousarray(np.asarray(v, f32)) for k, v in inputs.items()}
    x = np.ascontiguousarray(inp["L0_attn_x"].reshape(16384, D))
    p, nc = _get_prog("fused", build_fused)
    dfull = np.ascontiguousarray(
        np.broadcast_to(np.repeat(inp["L0_attn_D"], 128)[None, :], (128, D)))
    wxt = np.ascontiguousarray(inp["L0_attn_W_xproj"].T)
    in_maps = []
    for c in range(NCORES):
        dead, hselv, triv = _mamba_consts(c)
        pref = np.zeros((8, 1), f32)
        pref[:c] = 1.0
        in_maps.append({
            "x_in": x[c * NTOK:(c + 1) * NTOK], **consts,
            "gate_w": inp["L0_ffn_gate_w"], "w1_0": inp["L0_ffn_w1"],
            "w2_0": inp["L0_ffn_w2"],
            "wxt": wxt, "dt_bias": inp["L0_attn_dt_bias"].reshape(8, 1),
            "A_log": inp["L0_attn_A_log"].reshape(8, 1), "dfull": dfull,
            "deadv": dead, "hsel": hselv, "tri": triv,
            "router_w": inp["L1_ffn_router_w"],
            "router_b": inp["L1_ffn_router_b"].reshape(8, 1),
            "b1": inp["L1_ffn_b1"], "b2": inp["L1_ffn_b2"],
            "w1_1": inp["L1_ffn_w1"], "w2_1": inp["L1_ffn_w2"],
            "prefmask": pref,
            "MQ": inp["L1_attn_Mq"], "MK": inp["L1_attn_Mk"],
            "MV": inp["L1_attn_Mv"],
        })
    res = run_bass_kernel_spmd(nc, in_maps, core_ids=list(range(NCORES)))
    out = np.concatenate([res.results[c]["x_out"] for c in range(NCORES)], 0)
    return np.ascontiguousarray(out).reshape(4, SEQ, D)


_PROG_CACHE = {}
_PROG_CACHE = {}


def _get_prog(name, builder):
    if name not in _PROG_CACHE:
        p = builder()
        _PROG_CACHE[name] = (p, p.build())
    return _PROG_CACHE[name]
